# version 03
# baseline (speedup 1.0000x reference)
import numpy as np
from contextlib import ExitStack

import concourse.bass as bass
import concourse.mybir as mybir
from concourse.bass_utils import run_bass_kernel_spmd

F32 = mybir.dt.float32
F32R = mybir.dt.float32r
BF16 = mybir.dt.bfloat16
AF = mybir.ActivationFunctionType
ALU = mybir.AluOpType

NCORE = 8
T = 1024
NT = 8
Q = 128
DM = 2048
DI = 4096
DS = 64
NH = 64
HD = 64
EIN = 8384
DFF = 8192
EPS = 1e-5
KT = 16
XCT = 32
W = 1028



_TRACKED = {
    "tensor_copy", "tensor_tensor", "tensor_add", "tensor_sub", "tensor_mul",
    "tensor_max", "tensor_scalar", "tensor_single_scalar", "tensor_scalar_add",
    "tensor_scalar_sub", "tensor_scalar_mul", "tensor_scalar_max", "tensor_scalar_min",
    "tensor_relu", "tensor_reduce", "reduce_max", "reduce_sum", "scalar_tensor_tensor",
    "tensor_tensor_scan", "reciprocal", "bn_stats", "bn_aggr", "select",
    "copy_predicated", "activation", "memset", "memzero", "iota", "affine_select",
    "add", "mul", "copy", "sqrt", "square", "sign",
}


class EngProxy:

    def __init__(self, eng, haz):
        self._e = eng
        self._h = haz

    def __getattr__(self, name):
        fn = getattr(self._e, name)
        if name not in _TRACKED:
            return fn

        def wrapped(*args, **kw):
            aps = [a for a in args if isinstance(a, bass.AP)]
            for k, v in kw.items():
                if isinstance(v, bass.AP):
                    aps.append(v) if k != "accum_out" else None
            outs = set()
            ins = set()
            if aps:
                outs.add(aps[0].tensor.name)
                for a in aps[1:]:
                    ins.add(a.tensor.name)
            ao = kw.get("accum_out")
            if isinstance(ao, bass.AP):
                outs.add(ao.tensor.name)
            h = self._h
            if (h["w"] & (ins | outs)) or (h["r"] & outs):
                self._e.drain()
                h["w"].clear()
                h["r"].clear()
            h["w"] |= outs
            h["r"] |= ins
            return fn(*args, **kw)
        return wrapped


class Ctx:
    def __init__(self, nc, st):
        self.nc = nc
        self._pool = [st.enter_context(nc.semaphore(f"sem{i}")) for i in range(56)]
        self._next = 0
        self.block = st.enter_context(nc.Block())
        self.sem = {}
        for name in ("sT", "sV", "sS", "sG", "sCC"):
            self._get(name)
        self.ps = st.enter_context(nc.psum_tensor("ps", [128, 4096], F32))
        self.bw = [None] * 8
        self.br = [None] * 8

    def _get(self, name):
        if name not in self.sem:
            assert self._next < len(self._pool), "semaphore pool exhausted"
            self.sem[name] = [self._pool[self._next], 0]
            self._next += 1
        return self.sem[name]

    def inc(self, instr, name, n=1):
        e = self._get(name)
        instr.then_inc(e[0], n)
        e[1] += n
        return (name, e[1])

    def dinc(self, instr, name):
        return self.inc(instr, name, 16)

    def wait(self, eng, mark):
        if mark is not None:
            eng.wait_ge(self.sem[mark[0]][0], mark[1])

    def cur(self, name):
        return (name, self._get(name)[1])

    def bank(self, b, off=0, width=512, parts=128, p0=0):
        return self.ps[p0:p0 + parts, b * 512 + off: b * 512 + off + width]

    def pre_write(self, eng, banks):
        for b in banks:
            self.wait(eng, self.br[b])

    def set_writer(self, banks, mark):
        for b in banks:
            self.bw[b] = mark

    def pre_read(self, eng, banks):
        for b in banks:
            self.wait(eng, self.bw[b])

    def set_reader(self, banks, mark):
        for b in banks:
            self.br[b] = mark


def build_program(upto="full", dump=()):
    nc = bass.Bass("TRN2", target_bir_lowering=False, debug=False)
    dump = set(dump)

    def dram(name, shape, dtype, kind=None):
        if kind is None:
            kind = "ExternalOutput" if name in dump else "Internal"
        if kind == "Internal":
            return nc.dram_tensor(name, shape, dtype).ap()
        return nc.dram_tensor(name, shape, dtype, kind=kind).ap()

    xseg = dram("xseg", [3 + T, DM], F32, kind="ExternalInput")
    win_t = dram("win_t", [DM, EIN + NH], F32R, kind="ExternalInput")
    wout_t = dram("wout_t", [DI, DM], F32, kind="ExternalInput")
    wg_p = dram("wg_p", [DM, DFF], F32R, kind="ExternalInput")
    wu_p = dram("wu_p", [DM, DFF], F32R, kind="ExternalInput")
    wd_p = dram("wd_p", [DFF, DM], F32, kind="ExternalInput")
    convw = dram("convw", [128, 34 * 4], F32, kind="ExternalInput")
    convb = dram("convb", [128, 34], F32, kind="ExternalInput")
    rmsw = dram("rmsw", [128, KT], F32, kind="ExternalInput")
    normw = dram("normw", [128, DI // 128], F32, kind="ExternalInput")
    dtb = dram("dtb", [NH, 1], F32, kind="ExternalInput")
    alog = dram("alog", [NH, 1], F32, kind="ExternalInput")
    dvec = dram("dvec", [NH, 1], F32, kind="ExternalInput")
    cmask = dram("cmask", [1, NCORE], F32, kind="ExternalInput")
    out = dram("out", [T, DM], F32, kind="ExternalOutput")

    z_dram = dram("z_dram", [T, DI], BF16)
    xmid_dram = dram("xmid_dram", [T, DM], F32)
    cum_dram = dram("cum_dram", [NH, T], F32)
    ynorm_dram = dram("ynorm_dram", [XCT, 128, T], BF16)
    cc_in = dram("cc_in", [NH + 1, DI], F32)
    cc_out = dram("cc_out", [NCORE * (NH + 1), DI], F32)

    d_uT = dram("d_uT", [128, KT * W], F32) if "d_uT" in dump else None
    d_xbc = dram("d_xbc", [128, XCT * W], BF16) if "d_xbc" in dump else None
    d_bc = dram("d_bc", [64, 2 * W], BF16) if "d_bc" in dump else None
    d_dt = dram("d_dt", [NH, T], F32) if "d_dt" in dump else None
    d_cum = dram("d_cum", [NH, T], F32) if "d_cum" in dump else None
    d_sb = dram("d_sb", [NH, 2 * DI], F32) if "d_sb" in dump else None
    d_y = dram("d_y", [T, DI], F32) if "d_y" in dump else None

    st = ExitStack()
    with st:
        c = Ctx(nc, st)
        block = c.block
        _haz = {"v": {"w": set(), "r": set()}, "s": {"w": set(), "r": set()},
                "g": {"w": set(), "r": set()}}
        _bvec, _bsca, _bgps = block.vector, block.scalar, block.gpsimd
        block = type("B", (), {})()
        block.sync = c.block.sync
        block.tensor = c.block.tensor
        block.vector = lambda f: _bvec(lambda e: f(EngProxy(e, _haz["v"])))
        block.scalar = lambda f: _bsca(lambda e: f(EngProxy(e, _haz["s"])))
        block.gpsimd = lambda f: _bgps(lambda e: f(EngProxy(e, _haz["g"])))

        def sbL(name, shape, dt_, stk=st):
            return stk.enter_context(nc.sbuf_tensor(name, shape, dt_, side="left"))

        def sbR(name, shape, dt_, stk=st):
            return stk.enter_context(nc.sbuf_tensor(name, shape, dt_, side="right"))

        ident = sbL("ident", [128, 128], F32)
        identB = sbL("identB", [128, 128], BF16)
        ones = sbL("ones", [128, 128], F32)
        causal = sbL("causal", [128, 128], F32)
        cw_sb = sbL("cw_sb", [128, 34 * 4], F32)
        cb_sb = sbL("cb_sb", [128, 34], F32)
        rmsw_sb = sbL("rmsw_sb", [128, KT], F32)
        normw_sb = sbL("normw_sb", [128, DI // 128], F32)
        dtb_sb = sbL("dtb_sb", [NH, 1], F32)
        alog_sb = sbL("alog_sb", [NH, 1], F32)
        negA_sb = sbL("negA_sb", [NH, 1], F32)
        dmat = sbL("dmat", [128, NH], F32)
        cmrep = sbL("cmrep", [NH, NCORE], F32)
        statc = sbL("statc", [128, 4 * (NT + 1) + 4], F32)
        colw = sbL("colw", [128, 32], F32)
        bnst = sbL("bnst", [128, 52], F32)

        m0 = {}

        def p0_sync(q):
            last = None
            for dst, src in ((cw_sb, convw), (cb_sb, convb), (rmsw_sb, rmsw),
                             (normw_sb, normw), (dtb_sb, dtb), (alog_sb, alog)):
                last = c.dinc(q.dma_start(out=dst[:, :], in_=src[:, :]), "dP")
            last = c.dinc(q.dma_start(out=dmat[:, :],
                                      in_=dvec[:, 0].unsqueeze(0).broadcast_to([128, NH])), "dP")
            m0["params"] = c.dinc(q.dma_start(
                out=cmrep[:, :], in_=cmask[0, :].unsqueeze(0).broadcast_to([NH, NCORE])), "dP")
        block.sync(p0_sync)

        def p0_g(g):
            mk = c.inc(g.memset(ones[:, :], 1.0), "sG")
            c.wait(g, mk)
            c.inc(g.affine_select(ident[:, :], ones[:, :], [[0, 1], [0, 1], [0, 1], [-1, 128]],
                                  ALU.is_equal, 0.0, base=0, channel_multiplier=1), "sG")
            c.inc(g.affine_select(identB[:, :], ones[:, :], [[0, 1], [0, 1], [0, 1], [-1, 128]],
                                  ALU.is_equal, 0.0, base=0, channel_multiplier=1), "sG")
            m0["const"] = c.inc(g.affine_select(causal[:, :], ones[:, :],
                                                [[0, 1], [0, 1], [0, 1], [1, 128]],
                                                ALU.is_ge, 0.0, base=0, channel_multiplier=-1), "sG")
        block.gpsimd(p0_g)

        def p0_s(s):
            c.wait(s, m0["params"])
            m0["negA0"] = c.inc(s.activation(negA_sb[:, :], alog_sb[:, :], AF.Exp), "sS")
        block.scalar(p0_s)

        def p0_v(v):
            c.wait(v, m0["negA0"])
            m0["negA"] = c.inc(v.tensor_scalar_mul(negA_sb[:, :], negA_sb[:, :], -1.0), "sV")
        block.vector(p0_v)

        def ln_stats(v, src, P, width):
            nchunks = (width + 511) // 512
            for i in range(nchunks):
                w0 = min(512, width - i * 512)
                v.bn_stats(bnst[:P, i * 6:(i + 1) * 6], src[:, i * 512:i * 512 + w0])
            return v.bn_aggr(bnst[:P, 48:50], bnst[:P, 0:6 * nchunks])

        stB = ExitStack()
        uT = sbR("uT", [128, KT * W], F32R, stB)
        xst = sbR("xst", [128, 2 * DM], F32, stB)
        wstat = sbR("wstat", [128, 2 * 8 * 128], F32R, stB)
        stBq = ExitStack()
        dtsb = sbR("dtsb", [NH, T], F32, stBq)
        cum = sbR("cum", [NH, T], F32, stBq)
        lasb = sbR("lasb", [NH, T], F32, stBq)
        wloc = sbR("wloc", [NH, T], F32, stBq)
        acc = sbR("acc", [128, 1024], F32, stBq)

        stC = ExitStack()
        xbc = sbL("xbc", [128, XCT * W], BF16, stC)
        bbuf = sbL("bbuf", [64, W], BF16, stC)
        cbuf = sbL("cbuf", [64, W], BF16, stC)
        glog = sbL("glog", [NH, NT + 1], F32, stC)
        gcol = sbL("gcol", [NH, NT], F32, stC)
        ncumTr = sbL("ncumTr", [128, NT * NH], F32, stC)
        dtTr = sbL("dtTr", [128, NT * NH], F32, stC)
        wTr = sbL("wTr", [128, NT * NH], F32, stC)
        wgTr = sbL("wgTr", [128, NT * NH], F32, stC)
        btr = sbL("btr", [128, DS], BF16, stC)
        xseq = sbL("xseq", [128, DI], BF16, stC)
        xw = sbL("xw", [128, DI], BF16, stC)

        def pz_g(g):
            c.inc(g.memset(uT[:, :].bitcast(F32), 0.0), "sG")
            c.inc(g.memset(xbc[:, :], 0.0), "sG")
            c.inc(g.memset(bbuf[:, :], 0.0), "sG")
            m0["bufz"] = c.inc(g.memset(cbuf[:, :], 0.0), "sG")
        block.gpsimd(pz_g)

        uT_marks = []
        u_marks = {}
        for t in range(NT + 1):
            P = 128 if t < NT else 3
            par = t % 2
            base = 4 * t
            m1 = {}

            def p1_sync(q, t=t, par=par, P=P):
                if t >= 2:
                    c.wait(q, uT_marks[t - 2])
                src = xseg[3 + 128 * t: 3 + 128 * (t + 1), :] if t < NT else xseg[0:3, :]
                m1["x"] = c.dinc(q.dma_start(out=xst[:P, par * DM: par * DM + DM], in_=src), f"dX{t % 2}")
            block.sync(p1_sync)

            def p1_v(v, P=P, par=par, base=base, t=t):
                xin = xst[:P, par * DM: par * DM + DM]
                c.wait(v, m1["x"])
                if t >= 1:
                    c.wait(v, u_marks[t - 1])
                ln_stats(v, xin, P, DM)
                v.tensor_copy(statc[:P, base:base + 1], bnst[:P, 48:49])
                v.tensor_scalar(colw[:P, 2:3], bnst[:P, 49:50], EPS, None, ALU.add)
                v.reciprocal(colw[:P, 3:4], colw[:P, 2:3])
                v.tensor_mul(colw[:P, 4:5], bnst[:P, 49:50], colw[:P, 3:4])
                v.tensor_scalar(colw[:P, 4:5], colw[:P, 4:5], EPS, None, ALU.add)
                v.reciprocal(colw[:P, 5:6], colw[:P, 4:5])
                v.tensor_mul(colw[:P, 6:7], colw[:P, 3:4], colw[:P, 5:6])
                m1["cols"] = c.inc(v.tensor_copy(statc[:P, base + 1:base + 2], colw[:P, 3:4]), "sV")
            block.vector(p1_v)

            def p1_s(s, P=P, par=par, base=base, t=t):
                xin = xst[:P, par * DM: par * DM + DM]
                c.wait(s, m1["cols"])
                s.activation(statc[:P, base + 2:base + 3], colw[:P, 6:7], AF.Sqrt)
                s.activation(statc[:P, base + 1:base + 2], statc[:P, base + 1:base + 2], AF.Sqrt)
                s.activation(colw[:P, 7:8], statc[:P, base:base + 1], AF.Identity,
                             scale=statc[:P, base + 2:base + 3])
                s.activation(statc[:P, base + 3:base + 4], colw[:P, 7:8], AF.Identity, scale=-1.0)
                m1["u"] = c.inc(s.activation(xin, xin, AF.Identity,
                                             scale=statc[:P, base + 2:base + 3],
                                             bias=statc[:P, base + 3:base + 4]), "sS")
                u_marks[t] = m1["u"]
            block.scalar(p1_s)

            def p1_t(te, P=P, t=t, par=par):
                if t == 0:
                    c.wait(te, m0["const"])
                c.wait(te, m1["u"])
                xin = xst[:P, par * DM: par * DM + DM]
                for k in range(KT):
                    i = t * KT + k
                    b = i % 4
                    off = (i // 4) % 4 * 128
                    c.pre_write(te, [b])
                    mm = te.transpose(c.bank(b, off, P), xin[:, k * 128:(k + 1) * 128],
                                      ident[:P, :P])
                    c.set_writer([b], c.inc(mm, "sT"))
                    m1[("tr", k)] = c.cur("sT")
            block.tensor(p1_t)

            def p1_v2(v, P=P, t=t):
                if t == 0:
                    c.wait(v, m0["params"])
                    c.wait(v, m0["bufz"])
                c.wait(v, m1[("tr", KT - 1)])
                last = None
                for k in range(KT):
                    i = t * KT + k
                    b = i % 4
                    off = (i // 4) % 4 * 128
                    dstc = k * W + (128 * t if t < NT else 1024)
                    width = 128 if t < NT else 3
                    v.tensor_copy(uT[:, dstc:dstc + width], c.bank(b, off, width))
                    mm = v.tensor_scalar(uT[:, dstc:dstc + width], uT[:, dstc:dstc + width],
                                         rmsw_sb[:, k:k + 1], None, ALU.mult)
                    last = c.inc(mm, "sV")
                c.set_reader([0, 1, 2, 3], last)
                uT_marks.append(last)
            block.vector(p1_v2)
        m_uT_done = uT_marks[-1]
        if upto == "p1":
            _tail_stub(c, block, out, ident)
            stC.close()
            stBq.close()
            stB.close()
            return nc

        if "d_uT" in dump:
            def d1(q):
                c.wait(q, m_uT_done)
                c.dinc(q.dma_start(out=d_uT[:, :], in_=uT[:, :].bitcast(F32)), "dD")
            block.sync(d1)

        ets = [("x", j, 128, DI + 128 * j, 128, j) for j in range(XCT)]
        ets += [("b", 0, 64, 2 * DI, 128, 32), ("c", 0, 64, 2 * DI + 64, 128, 33),
                ("dt", 0, 64, 2 * DI + 128, 128, -1)]
        wr = win_t.rearrange("(a p) e -> p a e", p=128)
        tm_et = {}
        conv_mark = {}
        m_dt_ev = None
        wst_marks = {}
        for i, (kind, j, P, e0, ew, cwcol) in enumerate(ets):
            m2 = {}

            def p2_sync(q, i=i, e0=e0, ew=ew):
                for half in range(2):
                    hidx = 2 * i + half
                    par = hidx % 2
                    if hidx >= 2:
                        c.wait(q, tm_et[hidx - 2])
                    dst = wstat[:, par * 8 * 128: par * 8 * 128 + 8 * ew]
                    mm = q.dma_start(out=dst.rearrange("p (a e) -> p a e", a=8),
                                     in_=wr[:, half * 8:(half + 1) * 8, e0:e0 + ew])
                    wst_marks[hidx] = c.dinc(mm, f"dW{hidx % 2}")
            block.sync(p2_sync)

            def p2_t(te, i=i, ew=ew):
                if i == 0:
                    c.wait(te, m_uT_done)
                ebank = 2 * (i % 2)
                hbank = 4 + (i % 2)
                banks = [ebank, ebank + 1, hbank]
                c.pre_write(te, banks)
                for half in range(2):
                    hidx = 2 * i + half
                    par = hidx % 2
                    c.wait(te, wst_marks[hidx])
                    for kk in range(8):
                        k = half * 8 + kk
                        stat = wstat[:, par * 8 * 128 + kk * ew: par * 8 * 128 + (kk + 1) * ew]
                        first, last = (k == 0), (k == KT - 1)
                        te.matmul(c.bank(ebank, 0, 512, parts=ew), stat,
                                  uT[:, k * W: k * W + 512], start=first, stop=last)
                        te.matmul(c.bank(ebank + 1, 0, 512, parts=ew), stat,
                                  uT[:, k * W + 512: k * W + 1024], start=first, stop=last)
                        mm = te.matmul(c.bank(hbank, 0, 4, parts=ew), stat,
                                       uT[:, k * W + 1024: k * W + 1028], start=first, stop=last)
                    mk = c.inc(mm, "sT")
                    tm_et[hidx] = mk
                c.set_writer(banks, tm_et[2 * i + 1])
            block.tensor(p2_t)

            def p2_s(s, kind=kind, j=j, P=P, i=i):
                if i == 0:
                    c.wait(s, m0["bufz"])
                ebank = 2 * (i % 2)
                hbank = 4 + (i % 2)
                banks = [ebank, ebank + 1, hbank]
                c.pre_read(s, banks)
                if kind == "x":
                    dst = xbc[:, j * W: j * W + W]
                elif kind == "b":
                    dst = bbuf
                elif kind == "c":
                    dst = cbuf
                else:
                    dst = None
                if dst is not None:
                    s.activation(dst[:P, 3:3 + 1024], c.ps[:P, ebank * 512: ebank * 512 + 1024],
                                 AF.Identity)
                    mm = s.activation(dst[:P, 0:3], c.bank(hbank, 0, 3, parts=P), AF.Identity)
                else:
                    mm = s.activation(dtsb[:NH, 0:1024], c.ps[:NH, ebank * 512: ebank * 512 + 1024],
                                      AF.Identity)
                mk = c.inc(mm, "sS")
                c.set_reader(banks, mk)
                m2["ev"] = mk
            block.scalar(p2_s)
            if kind == "dt":
                m_dt_ev = m2["ev"]

            if kind != "dt":
                def p2_v(v, kind=kind, j=j, P=P, cwcol=cwcol, i=i):
                    buf = xbc[:, j * W: j * W + W] if kind == "x" else (bbuf if kind == "b" else cbuf)
                    if i == 0:
                        c.wait(v, m0["params"])
                    c.wait(v, m2["ev"])
                    if i >= 1:
                        c.wait(v, conv_mark[i - 1])
                    v.tensor_scalar(acc[:P, :], buf[:P, 0:1024],
                                    cw_sb[:P, cwcol * 4: cwcol * 4 + 1], None, ALU.mult)
                    for kk in range(1, 4):
                        v.scalar_tensor_tensor(acc[:P, :], buf[:P, kk:kk + 1024],
                                               cw_sb[:P, cwcol * 4 + kk: cwcol * 4 + kk + 1],
                                               acc[:P, :], ALU.mult, ALU.add)
                    m2["cv"] = c.inc(v.tensor_copy(colw[:P, 16:17], acc[:P, 0:1]), "sV")
                block.vector(p2_v)

                def p2_s2(s, kind=kind, j=j, P=P, cwcol=cwcol, i=i):
                    buf = xbc[:, j * W: j * W + W] if kind == "x" else (bbuf if kind == "b" else cbuf)
                    c.wait(s, m2["cv"])
                    mm = s.activation(buf[:P, 3:3 + 1024], acc[:P, :],
                                      AF.Silu, bias=cb_sb[:P, cwcol:cwcol + 1])
                    conv_mark[i] = c.inc(mm, "sS")
                block.scalar(p2_s2)
        m_conv_done = conv_mark[len(ets) - 2]

        if "d_xbc" in dump:
            def d2(q):
                c.wait(q, m_conv_done)
                c.dinc(q.dma_start(out=d_xbc[:, :], in_=xbc[:, :]), "dD")
                c.dinc(q.dma_start(out=d_bc[:, 0:W], in_=bbuf[:, :]), "dD")
                c.dinc(q.dma_start(out=d_bc[:, W:2 * W], in_=cbuf[:, :]), "dD")
            block.sync(d2)

        mdt = {}

        def pdt_v0(v):
            c.wait(v, m_dt_ev)
            mdt["dtb"] = c.inc(v.tensor_scalar(dtsb[:, :], dtsb[:, :], dtb_sb[:, :],
                                               None, ALU.add), "sV")
        block.vector(pdt_v0)

        def pdt_s(s):
            c.wait(s, mdt["dtb"])
            c.wait(s, m0["negA"])
            s.activation(lasb[:, :], dtsb[:, :], AF.Abs)
            s.activation(lasb[:, :], lasb[:, :], AF.Exp, scale=-1.0)
            mdt["a"] = c.inc(s.activation(cum[:, :], dtsb[:, :], AF.Relu), "sS")
        block.scalar(pdt_s)

        def pdt_v(v):
            c.wait(v, mdt["a"])
            mdt["b"] = c.inc(v.tensor_scalar(lasb[:, :], lasb[:, :], 1.0, None, ALU.add), "sV")
        block.vector(pdt_v)

        def pdt_s2(s):
            c.wait(s, mdt["b"])
            mdt["c"] = c.inc(s.activation(lasb[:, :], lasb[:, :], AF.Ln), "sS")
        block.scalar(pdt_s2)

        def pdt_v2(v):
            c.wait(v, mdt["c"])
            v.tensor_add(dtsb[:, :], lasb[:, :], cum[:, :])
            v.tensor_scalar(lasb[:, :], dtsb[:, :], negA_sb[:, :], None, ALU.mult)
            for ch in range(NT):
                v.tensor_tensor_scan(cum[:, ch * Q:(ch + 1) * Q], ones[:NH, :],
                                     lasb[:, ch * Q:(ch + 1) * Q], 0.0, ALU.mult, ALU.add)
            v.tensor_scalar(glog[:, 0:1], ones[:NH, 0:1], 0.0, None, ALU.mult)
            for ch in range(NT):
                v.tensor_add(glog[:, ch + 1:ch + 2], glog[:, ch:ch + 1],
                             cum[:, ch * Q + Q - 1: ch * Q + Q])
            for ch in range(NT):
                v.tensor_sub(gcol[:, ch:ch + 1], glog[:, NT:NT + 1], glog[:, ch:ch + 1])
            mdt["cum"] = c.inc(v.tensor_copy(colw[:NH, 17:18], glog[:, NT:NT + 1]), "sV")
        block.vector(pdt_v2)

        def pdt_sync(q):
            c.wait(q, mdt["cum"])
            mdt["cumd"] = c.dinc(q.dma_start(out=cum_dram[:, :], in_=cum[:, :]), "dCM")
        block.sync(pdt_sync)

        def pdt_s3(s):
            c.wait(s, mdt["cum"])
            for ch in range(NT):
                cs = slice(ch * Q, (ch + 1) * Q)
                s.activation(wloc[:, cs], cum[:, cs], AF.Exp, scale=-1.0,
                             bias=cum[:, ch * Q + Q - 1: ch * Q + Q])
                s.activation(lasb[:, cs], cum[:, cs], AF.Exp, scale=-1.0,
                             bias=gcol[:, ch:ch + 1])
            mdt["wx"] = c.inc(s.activation(colw[:1, 18:19], ones[:1, 0:1], AF.Identity), "sS")
        block.scalar(pdt_s3)

        def pdt_v3(v):
            c.wait(v, mdt["wx"])
            v.tensor_mul(wloc[:, :], wloc[:, :], dtsb[:, :])
            mdt["w"] = c.inc(v.tensor_mul(lasb[:, :], lasb[:, :], dtsb[:, :]), "sV")
        block.vector(pdt_v3)

        for ch in range(NT):
            mtr = {}

            def ptr_t(te, ch=ch):
                c.wait(te, mdt["w"])
                c.pre_write(te, [6])
                cs = slice(ch * Q, (ch + 1) * Q)
                te.transpose(c.bank(6, 0, NH), cum[:, cs], ident[:NH, :NH])
                te.transpose(c.bank(6, 64, NH), dtsb[:, cs], ident[:NH, :NH])
                te.transpose(c.bank(6, 128, NH), wloc[:, cs], ident[:NH, :NH])
                mm = te.transpose(c.bank(6, 192, NH), lasb[:, cs], ident[:NH, :NH])
                mk = c.inc(mm, "sT")
                c.set_writer([6], mk)
                mtr["t"] = mk
            block.tensor(ptr_t)

            def ptr_v(v, ch=ch):
                c.pre_read(v, [6])
                v.tensor_scalar(ncumTr[:, ch * NH:(ch + 1) * NH], c.bank(6, 0, NH), -1.0, None, ALU.mult)
                v.tensor_copy(dtTr[:, ch * NH:(ch + 1) * NH], c.bank(6, 64, NH))
                v.tensor_copy(wTr[:, ch * NH:(ch + 1) * NH], c.bank(6, 128, NH))
                mm = v.tensor_copy(wgTr[:, ch * NH:(ch + 1) * NH], c.bank(6, 192, NH))
                mk = c.inc(mm, "sV")
                c.set_reader([6], mk)
            block.vector(ptr_v)
        m_tr_done = c.cur("sV")
        if upto == "p2a":
            _tail_stub(c, block, out, ident)
            stC.close()
            stBq.close()
            stB.close()
            return nc
        stBq.close()

        if "d_dt" in dump:
            def d3(q):
                c.wait(q, mdt["cum"])
                c.dinc(q.dma_start(out=d_dt[:, :], in_=dtsb[:, :]), "dD")
                c.dinc(q.dma_start(out=d_cum[:, :], in_=cum[:, :]), "dD")
            block.sync(d3)

        stL8 = ExitStack()
        l8st = sbL("l8st", [NH, DI], F32, stL8)

        def xtranspose(ch, dst_bank, xw_src):
            xwm = {}

            def xt_t(te):
                c.pre_write(te, [dst_bank])
                mm = te.transpose(c.bank(dst_bank, 0, 32).bitcast(BF16),
                                  bbuf[:, 3 + ch * Q: 3 + (ch + 1) * Q], identB[:DS, :DS])
                c.set_writer([dst_bank], c.inc(mm, "sT"))
            block.tensor(xt_t)

            def xt_v(v):
                c.pre_read(v, [dst_bank])
                mm = v.tensor_copy(btr[:, :], c.bank(dst_bank, 0, 32).bitcast(BF16))
                mk = c.inc(mm, "sV")
                c.set_reader([dst_bank], mk)
            block.vector(xt_v)

            for grp in range(4):
                def xg_t(te, grp=grp):
                    c.pre_write(te, [dst_bank])
                    for u8 in range(8):
                        jj = grp * 8 + u8
                        mm = te.transpose(c.bank(dst_bank, u8 * 64, 64).bitcast(BF16),
                                          xbc[:, jj * W + 3 + ch * Q: jj * W + 3 + (ch + 1) * Q],
                                          identB[:, :])
                    c.set_writer([dst_bank], c.inc(mm, "sT"))
                block.tensor(xg_t)

                def xg_v(v, grp=grp):
                    c.pre_read(v, [dst_bank])
                    v.tensor_copy(xseq[:, grp * 1024:(grp + 1) * 1024],
                                  c.bank(dst_bank, 0, 512).bitcast(BF16))
                    for h in range(grp * 16, (grp + 1) * 16):
                        mm = v.tensor_scalar(xw[:, h * HD:(h + 1) * HD], xseq[:, h * HD:(h + 1) * HD],
                                             xw_src[:, ch * NH + h: ch * NH + h + 1], None, ALU.mult)
                    mk = c.inc(mm, "sV")
                    c.set_reader([dst_bank], mk)
                    xwm[grp] = mk
                block.vector(xg_v)
            return xwm

        for ch in range(NT):
            if ch == 0:
                def c0_t(te):
                    c.wait(te, m_conv_done)
                    c.wait(te, m_tr_done)
                block.tensor(c0_t)
            xwm = xtranspose(ch, 6, wgTr)
            for qt in range(4):
                def st_t(te, ch=ch, qt=qt):
                    c.wait(te, xwm[qt])
                    c.pre_write(te, [7])
                    for i2 in range(16):
                        h = qt * 16 + i2
                        mm = te.matmul(c.bank(7, i2 * 32, 64, parts=64).bitcast(F32)
                                       if False else c.bank(7, (i2 % 8) * 64, 64, parts=64),
                                       btr[:, :], xw[:, h * HD:(h + 1) * HD],
                                       start=(i2 % 8 == 0 and True), stop=True) \
                            if False else None
                    c.set_writer([7], None)
                block.tensor(st_t) if False else None

                for sq in range(2):
                    def st_t2(te, ch=ch, qt=qt, sq=sq):
                        c.wait(te, xwm[qt])
                        c.pre_write(te, [7])
                        for i2 in range(8):
                            h = qt * 16 + sq * 8 + i2
                            mm = te.matmul(c.bank(7, i2 * 64, 64, parts=64), btr[:, :],
                                           xw[:, h * HD:(h + 1) * HD],
                                           start=(i2 == 0), stop=(i2 == 7))
                        c.set_writer([7], c.inc(mm, "sT"))
                    block.tensor(st_t2)

                    def st_v2(v, ch=ch, qt=qt, sq=sq):
                        if ch == 0 and qt == 0 and sq == 0:
                            c.wait(v, mdt["cumd"])
                            if dump:
                                c.wait(v, c.cur("dD"))
                        c.pre_read(v, [7])
                        dst = l8st[:, (qt * 2 + sq) * 512:(qt * 2 + sq + 1) * 512]
                        if ch == 0:
                            mm = v.tensor_copy(dst, c.bank(7, 0, 512, parts=64))
                        else:
                            mm = v.tensor_add(dst, dst, c.bank(7, 0, 512, parts=64))
                        mk = c.inc(mm, "sV")
                        c.set_reader([7], mk)
                    block.vector(st_v2)
        mdt["l8"] = c.cur("sV")

        def cc_sync(q):
            c.wait(q, mdt["l8"])
            c.dinc(q.dma_start(out=cc_in[0:NH, :], in_=l8st[:, :]), "dCI")
            c.dinc(q.dma_start(out=cc_in[NH:NH + 1, NH:].rearrange("a (b e) -> a b e", b=63),
                               in_=ones[0:1, 0:NH].unsqueeze(1).broadcast_to([1, 63, NH])), "dCI")
            mdt["ccin"] = c.dinc(q.dma_start(out=cc_in[NH:NH + 1, 0:NH], in_=glog[:, NT:NT + 1]), "dCI")
            if d_sb is not None:
                c.dinc(q.dma_start(out=d_sb[:, 0:DI], in_=l8st[:, :]), "dD")
        block.sync(cc_sync)

        if upto == "p2s":
            _tail_stub(c, block, out, ident)
            stL8.close()
            stC.close()
            stB.close()
            return nc

        def cc_g(g):
            c.wait(g, mdt["ccin"])
            mm = g.collective_compute("AllGather", ALU.bypass,
                                      replica_groups=[list(range(NCORE))],
                                      ins=[cc_in[:, :].opt()], outs=[cc_out[:, :].opt()])
            mdt["cc"] = c.inc(mm, "sCC")
        block.gpsimd(cc_g)
        stL8.close()

        zW = sbR("zW", [128, 2 * 512], F32R, stB)
        zstg = sbR("zstg", [128, 4 * 512], BF16, stB)
        stg_rd = {}
        m_z_done = None
        zdma_prev = None
        for zc in range(NT):
            mz = {}
            for k in range(KT):
                par = k % 2
                mzk = {}

                def pz_sync(q, zc=zc, k=k, par=par):
                    if (zc * KT + k) < 2:
                        c.wait(q, m_tr_done)
                        c.wait(q, mdt["cumd"])
                        c.wait(q, mdt["ccin"])
                        if dump:
                            c.wait(q, c.cur("dD"))
                    else:
                        c.wait(q, stg_rd[zc * KT + k - 2])
                    mm = q.dma_start(out=zW[:, par * 512:(par + 1) * 512],
                                     in_=win_t[:, zc * 512:(zc + 1) * 512]
                                     .rearrange("(a p) e -> p a e", p=128)[:, k, :])
                    mzk["w"] = c.dinc(mm, f"dZ{k % 2}")
                (block.sync if k % 2 == 0 else block.scalar)(pz_sync)

                def pz_t(te, zc=zc, k=k, par=par):
                    c.wait(te, mzk["w"])
                    if k == 0:
                        c.pre_write(te, list(range(8)))
                    for l in range(NT):
                        mm = te.matmul(c.bank(l, 0, 512),
                                       uT[:, k * W + l * 128: k * W + l * 128 + 128],
                                       zW[:, par * 512:(par + 1) * 512],
                                       start=(k == 0), stop=(k == KT - 1))
                    mk = c.inc(mm, "sT")
                    stg_rd[zc * KT + k] = mk
                    if k == KT - 1:
                        c.set_writer(list(range(8)), mk)
                block.tensor(pz_t)

            for hf in range(2):
                def pz_s(s, zc=zc, hf=hf):
                    if zc == 0 and hf == 0:
                        c.wait(s, mdt["ccin"])
                    if zdma_prev is not None:
                        c.wait(s, zdma_prev)
                    c.pre_read(s, [4 * hf + i for i in range(4)])
                    for l in range(4):
                        mm = s.activation(zstg[:, l * 512:(l + 1) * 512],
                                          c.bank(4 * hf + l, 0, 512), AF.Identity)
                    mk = c.inc(mm, "sS")
                    c.set_reader([4 * hf + i for i in range(4)], mk)
                    mz[("dr", hf)] = mk
                block.scalar(pz_s)

                def pz_sync2(q, zc=zc, hf=hf):
                    c.wait(q, mz[("dr", hf)])
                    mm = q.dma_start(out=z_dram[hf * 512:(hf + 1) * 512, zc * 512:(zc + 1) * 512]
                                     .rearrange("(a p) e -> p a e", p=128),
                                     in_=zstg[:, :].rearrange("p (a e) -> p a e", a=4))
                    mz[("zd", hf)] = c.dinc(mm, "dZO")
                block.sync(pz_sync2)
                zdma_prev = mz[("zd", hf)]
            m_z_done = zdma_prev

        if upto == "p2":
            _tail_stub(c, block, out, ident)
            stC.close()
            stB.close()
            return nc
        stB.close()

        sbefore = sbL("sbefore", [NH, DI], F32, stC)
        sb16 = sbL("sb16", [NH, DI], BF16, stC)
        ed16 = sbL("ed16", [128, 2 * 16 * 128], F32, stC)
        stEC = ExitStack()
        eseg = sbL("eseg", [NH, 2 * DI], F32, stEC)
        almat = sbL("almat", [NH, NCORE * NH], F32, stEC)
        msum = sbL("msum", [NH, NH], F32, stEC)
        wexp = sbL("wexp", [NH, NH], F32, stEC)

        def ec_sync(q):
            c.wait(q, mdt["cc"])
            c.wait(q, m_z_done)
            src = cc_out.rearrange("(a p) e -> a p e", p=NH + 1)[:, NH, 0:NH]
            mdt["al"] = c.dinc(q.dma_start(out=almat[:, :].rearrange("p (a e) -> p a e", a=NCORE),
                                           in_=src.unsqueeze(0).broadcast_to([NH, NCORE, NH])), "dAL")
        block.sync(ec_sync)

        def ec_v0(v):
            c.wait(v, m_z_done)
            c.wait(v, mdt["al"])
            mdt[("ms", NCORE)] = c.inc(v.tensor_scalar(msum[:, :], almat[:, 0:NH], 0.0,
                                                       None, ALU.mult), "sV")
        block.vector(ec_v0)

        eseg_marks = {}
        for j in range(NCORE - 1, -1, -1):
            mej = {}

            def ej_sync(q, j=j):
                if j <= NCORE - 3:
                    c.wait(q, eseg_marks[j + 2])
                c.wait(q, mdt["cc"])
                c.wait(q, m_z_done)
                mm = q.dma_start(out=eseg[:, (j % 2) * DI:(j % 2) * DI + DI],
                                 in_=cc_out[(NH + 1) * j:(NH + 1) * j + NH, :])
                mej["sg"] = c.dinc(mm, f"dES{j % 2}")
            block.sync(ej_sync)

            def ej_s(s, j=j):
                c.wait(s, mdt[("ms", j + 1)])
                mej["we"] = c.inc(s.activation(wexp[:, :], msum[:, :], AF.Exp), "sS")
            block.scalar(ej_s)

            def ej_v(v, j=j):
                if j == NCORE - 1:
                    c.wait(v, m_z_done)
                c.wait(v, mej["we"])
                c.wait(v, mej["sg"])
                v.tensor_scalar(wexp[:, :], wexp[:, :], cmrep[:, j:j + 1], None, ALU.mult)
                es = eseg[:, (j % 2) * DI:(j % 2) * DI + DI]
                e3 = es.rearrange("p (a e) -> p a e", a=NH)
                w3 = wexp[:, :].unsqueeze(2).broadcast_to([NH, NH, HD])
                if j == NCORE - 1:
                    v.tensor_tensor(sbefore[:, :].rearrange("p (a e) -> p a e", a=NH), e3, w3, ALU.mult)
                else:
                    v.tensor_tensor(ed16[:NH, 0:DI].rearrange("p (a e) -> p a e", a=NH), e3, w3, ALU.mult)
                    v.tensor_add(sbefore[:, :], sbefore[:, :], ed16[:NH, 0:DI])
                mm = v.scalar_tensor_tensor(msum[:, :], almat[:, j * NH:(j + 1) * NH],
                                            cmrep[:, j:j + 1], msum[:, :], ALU.mult, ALU.add)
                mk = c.inc(mm, "sV")
                mdt[("ms", j)] = mk
                eseg_marks[j] = mk
            block.vector(ej_v)

        m_E = c.cur("sV")

        def sb16_v(v):
            mdt["sb16"] = c.inc(v.tensor_copy(sb16[:, :], sbefore[:, :]), "sV")
        block.vector(sb16_v)
        stEC.close()

        if "d_sb" in dump:
            def d4(q):
                c.wait(q, mdt["sb16"])
                c.dinc(q.dma_start(out=d_sb[:, DI:2 * DI], in_=sbefore[:, :]), "dD")
            block.sync(d4)

        if upto == "p2c":
            _tail_stub(c, block, out, ident)
            stC.close()
            return nc

        stP3 = ExitStack()
        mtb = sbL("mtb", [128, 2 * 16 * 128], BF16, stP3)
        bcq = sbL("bcq", [128, 2 * 2048], F32, stP3)
        yall = sbL("yall", [128, DI], F32, stP3)
        tv = sbL("tv", [128, 512], F32, stP3)
        zsl = sbL("zsl", [128, DI], BF16, stP3)
        ynstg = sbL("ynstg", [128, XCT * 128], BF16, stP3)
        gmb = sbL("gmb", [128, 128], F32, stP3)
        eccTr = sbL("eccTr", [128, NT * NH], F32, stP3)
        amat = sbL("amat", [NH, NT * NH], F32, stP3)

        def p3_prep_s(s):
            c.wait(s, mdt["cumd"])
            for ch in range(NT):
                s.activation(eccTr[:, ch * NH:(ch + 1) * NH], ncumTr[:, ch * NH:(ch + 1) * NH],
                             AF.Exp, scale=-1.0)
            mdt["p3prep"] = c.inc(s.activation(colw[:1, 18:19], ones[:1, 0:1], AF.Identity), "sS")
        block.scalar(p3_prep_s)

        gseq = [(ch, g) for ch in range(NT) for g in range(4)]
        bcq_marks = {}
        bcqm_marks = {}
        sed_marks = {}

        def bcq_dma(q, gi):
            ch, g = gseq[gi]
            if gi >= 2:
                c.wait(q, sed_marks[gi - 2])
            src = cum_dram[g * 16:(g + 1) * 16, ch * Q:(ch + 1) * Q]
            mm = q.dma_start(out=bcq[:, (gi % 2) * 2048:(gi % 2) * 2048 + 2048]
                             .rearrange("p (a e) -> p a e", a=16),
                             in_=src.unsqueeze(0).broadcast_to([128, 16, Q]))
            bcq_marks[gi] = c.dinc(mm, f"dB{gi % 2}")

        def prime_sync(q):
            c.wait(q, mdt["sb16"])
            bcq_dma(q, 0)
            bcq_dma(q, 1)
        block.sync(prime_sync)

        zsl_marks = {}
        yn_marks = {}
        dy_marks = {}
        mtb_marks = {}
        ysg_t = {}
        ysg_v = {}
        for ch in range(NT):
            mc3 = {}

            def p3_zsync(q, ch=ch):
                if ch >= 1:
                    c.wait(q, yn_marks[ch - 1])
                c.wait(q, m_z_done)
                mm = q.dma_start(out=zsl[:, :], in_=z_dram[ch * 128:(ch + 1) * 128, :])
                zsl_marks[ch] = c.dinc(mm, "dZS")
            block.sync(p3_zsync)

            def p3_g(te, ch=ch):
                if ch == 0:
                    c.wait(te, mdt["p3prep"])
                c.pre_write(te, [2])
                mm = te.matmul(c.bank(2, 0, 128), bbuf[:, 3 + ch * Q: 3 + (ch + 1) * Q],
                               cbuf[:, 3 + ch * Q: 3 + (ch + 1) * Q], start=True, stop=True)
                c.set_writer([2], c.inc(mm, "sT"))
            block.tensor(p3_g)

            def p3_gv(v, ch=ch):
                if ch == 0:
                    c.wait(v, m0["const"])
                c.pre_read(v, [2])
                mm = v.tensor_tensor(gmb[:, :], c.bank(2, 0, 128), causal[:, :], ALU.mult)
                mk = c.inc(mm, "sV")
                c.set_reader([2], mk)
                mc3["gm"] = mk
            block.vector(p3_gv)

            xwm = xtranspose(ch, 3, wTr)

            for g in range(4):
                gi = ch * 4 + g


                def pg_gm(g_, ch=ch, g=g, gi=gi):
                    c.wait(g_, bcq_marks[gi])
                    if gi >= 2:
                        c.wait(g_, sed_marks[gi - 2])
                    mm = g_.affine_select(bcq[:, (gi % 2) * 2048:(gi % 2) * 2048 + 2048],
                                          bcq[:, (gi % 2) * 2048:(gi % 2) * 2048 + 2048],
                                          [[0, 1], [0, 1], [0, 16], [1, 128]],
                                          ALU.is_ge, -1e30, base=0, channel_multiplier=-1)
                    bcqm_marks[gi] = c.inc(mm, "sG")
                block.gpsimd(pg_gm)

                def pg_s(s, ch=ch, g=g, gi=gi):
                    c.wait(s, bcqm_marks[gi])
                    if gi >= 2:
                        c.wait(s, mtb_marks[gi - 2])
                    for i2 in range(16):
                        h = g * 16 + i2
                        s.activation(ed16[:, (gi % 2) * 2048 + i2 * 128:
                                          (gi % 2) * 2048 + (i2 + 1) * 128],
                                     bcq[:, (gi % 2) * 2048 + i2 * 128:
                                         (gi % 2) * 2048 + (i2 + 1) * 128],
                                     AF.Exp, bias=ncumTr[:, ch * NH + h: ch * NH + h + 1])
                    acol = bcq[0:NH, (gi % 2) * 2048:(gi % 2) * 2048 + 2048] \
                        .rearrange("p (a e) -> p a e", e=128)[:, :, 127]
                    mm = s.activation(amat[:, ch * NH + g * 16: ch * NH + (g + 1) * 16],
                                      acol, AF.Exp)
                    sed_marks[gi] = c.inc(mm, "sS")
                block.scalar(pg_s)

                def pg_sync(q, gi=gi):
                    if gi + 2 < len(gseq):
                        bcq_dma(q, gi + 2)
                block.sync(pg_sync)

                def pg_v(v, ch=ch, g=g, gi=gi):
                    c.wait(v, sed_marks[gi])
                    c.wait(v, mc3["gm"])
                    if gi >= 2:
                        c.wait(v, ysg_t[(gi - 2, 1)])
                    for i2 in range(16):
                        h = g * 16 + i2
                        mm = v.scalar_tensor_tensor(
                            mtb[:, (gi % 2) * 2048 + i2 * 128:(gi % 2) * 2048 + (i2 + 1) * 128],
                            gmb[:, :], dtTr[:, ch * NH + h: ch * NH + h + 1],
                            ed16[:, (gi % 2) * 2048 + i2 * 128:(gi % 2) * 2048 + (i2 + 1) * 128],
                            ALU.mult, ALU.mult)
                    mtb_marks[gi] = c.inc(mm, "sV")
                block.vector(pg_v)

                for sg in range(2):
                    sgi = gi * 2 + sg

                    def pg_t(te, ch=ch, g=g, gi=gi, sg=sg, sgi=sgi):
                        c.wait(te, mtb_marks[gi])
                        c.wait(te, xwm[g])
                        if ch == 0 and g == 0 and sg == 0:
                            c.wait(te, mdt["sb16"])
                        ya, yi = (4, 5) if sgi % 2 == 0 else (6, 7)
                        c.pre_write(te, [ya, yi])
                        for i2 in range(8):
                            h = g * 16 + sg * 8 + i2
                            te.matmul(c.bank(ya, i2 * 64, 64),
                                      mtb[:, (gi % 2) * 2048 + (sg * 8 + i2) * 128:
                                          (gi % 2) * 2048 + (sg * 8 + i2 + 1) * 128],
                                      xseq[:, h * HD:(h + 1) * HD], start=(i2 == 0), stop=(i2 == 7))
                        for i2 in range(8):
                            h = g * 16 + sg * 8 + i2
                            mm = te.matmul(c.bank(yi, i2 * 64, 64),
                                           cbuf[:, 3 + ch * Q: 3 + (ch + 1) * Q],
                                           sb16[:, h * HD:(h + 1) * HD], start=(i2 == 0), stop=(i2 == 7))
                        mk = c.inc(mm, "sT")
                        c.set_writer([ya, yi], mk)
                        ysg_t[(gi, sg)] = mk
                    block.tensor(pg_t)

                    def pg_ev(v, ch=ch, g=g, gi=gi, sg=sg, sgi=sgi):
                        if dump and g == 0 and sg == 0 and ch >= 1:
                            c.wait(v, dy_marks.get(ch - 1))
                        ya, yi = (4, 5) if sgi % 2 == 0 else (6, 7)
                        c.pre_read(v, [ya, yi])
                        h0 = g * 16 + sg * 8
                        ecc = eccTr[:, ch * NH + h0: ch * NH + h0 + 8]
                        v.tensor_tensor(tv[:, :].rearrange("p (a e) -> p a e", a=8),
                                        c.bank(yi, 0, 512).rearrange("p (a e) -> p a e", a=8),
                                        ecc.unsqueeze(2).broadcast_to([128, 8, HD]), ALU.mult)
                        mm = v.tensor_add(yall[:, h0 * HD: h0 * HD + 512],
                                          c.bank(ya, 0, 512), tv[:, :])
                        mk = c.inc(mm, "sV")
                        c.set_reader([ya, yi], mk)
                        ysg_v[(gi, sg)] = mk
                    block.vector(pg_ev)

            for sq in range(8):
                def sc_t(te, ch=ch, sq=sq):
                    bb = sq % 2
                    c.wait(te, xwm[sq // 2])
                    c.pre_write(te, [bb])
                    for i2 in range(8):
                        h = sq * 8 + i2
                        mm = te.matmul(c.bank(bb, i2 * 64, 64, parts=64), btr[:, :],
                                       xw[:, h * HD:(h + 1) * HD], start=(i2 == 0), stop=(i2 == 7))
                    c.set_writer([bb], c.inc(mm, "sT"))
                block.tensor(sc_t)

                def sc_v(v, ch=ch, sq=sq):
                    bb = sq % 2
                    if dump and ch == 0 and sq == 0:
                        c.wait(v, c.cur("dD"))
                    c.wait(v, ysg_t[(ch * 4 + 3, 1)])
                    c.pre_read(v, [bb])
                    sl = sbefore[:, sq * 512:(sq + 1) * 512]
                    am = amat[:, ch * NH + sq * 8: ch * NH + sq * 8 + 8]
                    v.tensor_tensor(tv[:NH, :].rearrange("p (a e) -> p a e", a=8),
                                    sl.rearrange("p (a e) -> p a e", a=8),
                                    am.unsqueeze(2).broadcast_to([NH, 8, HD]), ALU.mult)
                    mm = v.tensor_add(sl, tv[:NH, :], c.bank(bb, 0, 512, parts=64))
                    mk = c.inc(mm, "sV")
                    c.set_reader([bb], mk)
                block.vector(sc_v)

            def sb16_u(v, ch=ch):
                mm = v.tensor_copy(sb16[:, :], sbefore[:, :])
                c.inc(mm, "sV")
            block.vector(sb16_u)

            def tail_v(v, ch=ch):
                c.wait(v, zsl_marks[ch])
                v.tensor_tensor(xw[:, :].rearrange("p (a e) -> p a e", a=NH),
                                xseq[:, :].rearrange("p (a e) -> p a e", a=NH),
                                dmat[:, :].unsqueeze(2).broadcast_to([128, NH, HD]), ALU.mult)
                mc3["dx"] = c.inc(v.tensor_add(yall[:, :], yall[:, :], xw[:, :]), "sV")
            block.vector(tail_v)

            def tail_s(s, ch=ch):
                c.wait(s, mc3["dx"])
                mm = s.activation(zsl[:, :], zsl[:, :], AF.Silu)
                mc3["sz"] = c.inc(mm, "sS")
            block.scalar(tail_s)

            def tail_v2(v, ch=ch):
                c.wait(v, mc3["sz"])
                v.tensor_mul(yall[:, :], yall[:, :], zsl[:, :])
                ln_stats(v, yall[:, :], 128, DI)
                v.tensor_mul(colw[:, 20:21], bnst[:, 48:49], bnst[:, 48:49])
                v.tensor_add(colw[:, 20:21], colw[:, 20:21], bnst[:, 49:50])
                v.tensor_scalar(colw[:, 20:21], colw[:, 20:21], EPS, None, ALU.add)
                mm = v.reciprocal(colw[:, 21:22], colw[:, 20:21])
                mc3["ys"] = c.inc(mm, "sV")
            block.vector(tail_v2)

            def tail_s2(s, ch=ch):
                c.wait(s, mc3["ys"])
                mm = s.activation(colw[:, 22:23], colw[:, 21:22], AF.Sqrt)
                mc3["rs"] = c.inc(mm, "sS")
            block.scalar(tail_s2)

            def tail_v3(v, ch=ch):
                c.wait(v, mc3["rs"])
                mm = v.tensor_scalar(xw[:, :], yall[:, :], colw[:, 22:23], None, ALU.mult)
                mc3["ygb"] = c.inc(mm, "sV")
            block.vector(tail_v3)

            if "d_y" in dump:
                def dyq(q, ch=ch):
                    c.wait(q, mc3["ygb"])
                    dy_marks[ch] = c.dinc(q.dma_start(out=d_y[ch * 128:(ch + 1) * 128, :],
                                                      in_=yall[:, :]), "dD")
                block.sync(dyq)

            for grp in range(4):
                def yt_t(te, ch=ch, grp=grp):
                    c.wait(te, mc3["ygb"])
                    c.pre_write(te, [3])
                    for u8 in range(8):
                        jj = grp * 8 + u8
                        mm = te.transpose(c.bank(3, u8 * 64, 64).bitcast(BF16),
                                          xw[:, jj * 128:(jj + 1) * 128], identB[:, :])
                    c.set_writer([3], c.inc(mm, "sT"))
                block.tensor(yt_t)

                def yt_v(v, ch=ch, grp=grp):
                    if grp == 0 and ch >= 1:
                        c.wait(v, yn_marks[ch - 1])
                    c.pre_read(v, [3])
                    mm = v.tensor_copy(ynstg[:, grp * 1024:(grp + 1) * 1024],
                                       c.bank(3, 0, 512).bitcast(BF16))
                    mk = c.inc(mm, "sV")
                    c.set_reader([3], mk)
                    mc3[("yn", grp)] = mk
                block.vector(yt_v)

            def yn_sync(q, ch=ch):
                for grp in range(4):
                    c.wait(q, mc3[("yn", grp)])
                mm = q.dma_start(out=ynorm_dram.transpose([1, 0, 2])[:, :, ch * 128:(ch + 1) * 128],
                                 in_=ynstg[:, :].rearrange("p (a e) -> p a e", a=XCT))
                yn_marks[ch] = c.dinc(mm, "dYN")
            block.sync(yn_sync)
        m_p3_done = yn_marks[NT - 1]

        if upto == "p3":
            _tail_stub(c, block, out, ident)
            stP3.close()
            stC.close()
            return nc

        stP3.close()
        stC.close()

        x2T = sbL("x2T", [128, KT * 1024], F32R)
        stP4 = ExitStack()
        attn_acc = sbL("attn_acc", [128, NT * DM], F32, stP4)
        ar = sbL("ar", [128, DM], F32, stP4)
        xmd = sbL("xmd", [128, DM], F32, stP4)
        x2b = sbL("x2b", [128, DM], F32, stP4)
        xst4 = sbL("xst4", [128, 2 * DM], F32, stP4)
        ynst = sbL("ynst", [128, 2 * 1024], BF16, stP4)
        wost = sbL("wost", [128, 2 * 512], F32, stP4)
        wob = sbL("wob", [128, 2 * 512], BF16, stP4)

        yn_rd = {}
        wo_rd = {}
        dr_marks = {}
        for dch in range(4):
            for k in range(XCT):
                ki = dch * XCT + k
                m4 = {}

                def p4_sync(q, dch=dch, k=k, ki=ki):
                    if ki >= 2:
                        c.wait(q, yn_rd[ki - 2])
                    if ki == 0:
                        c.wait(q, m_p3_done)
                    mm = q.dma_start(out=ynst[:, (ki % 2) * 1024:(ki % 2) * 1024 + 1024],
                                     in_=ynorm_dram[k, :, :])
                    m4["yn"] = c.dinc(mm, f"dYR{ki % 2}")
                    if ki >= 2:
                        c.wait(q, wo_rd[ki - 2])
                    mm = q.dma_start(out=wost[:, (ki % 2) * 512:(ki % 2) * 512 + 512],
                                     in_=wout_t[k * 128:(k + 1) * 128, dch * 512:(dch + 1) * 512])
                    m4["wo"] = c.dinc(mm, f"dWO{ki % 2}")
                block.sync(p4_sync)

                def p4_g(g, k=k, ki=ki):
                    if ki == 0:
                        c.wait(g, m0["params"])
                    c.wait(g, m4["wo"])
                    if ki >= 2:
                        c.wait(g, wo_rd[ki - 2])
                    mm = g.tensor_scalar(wob[:, (ki % 2) * 512:(ki % 2) * 512 + 512],
                                         wost[:, (ki % 2) * 512:(ki % 2) * 512 + 512],
                                         normw_sb[:, k:k + 1], None, ALU.mult)
                    m4["wb"] = c.inc(mm, f"gW{ki % 2}")
                block.gpsimd(p4_g)

                def p4_t(te, dch=dch, k=k, ki=ki):
                    c.wait(te, m4["yn"])
                    c.wait(te, m4["wb"])
                    for l in range(NT):
                        if k == 0:
                            c.pre_write(te, [l])
                        mm = te.matmul(c.bank(l, 0, 512),
                                       ynst[:, (ki % 2) * 1024 + l * 128:(ki % 2) * 1024 + (l + 1) * 128],
                                       wob[:, (ki % 2) * 512:(ki % 2) * 512 + 512],
                                       start=(k == 0), stop=(k == XCT - 1))
                    mk = c.inc(mm, "sT")
                    yn_rd[ki] = mk
                    wo_rd[ki] = mk
                    if k == XCT - 1:
                        c.set_writer(list(range(8)), mk)
                block.tensor(p4_t)

            def p4_dr(s, dch=dch):
                for l in range(NT):
                    c.pre_read(s, [l])
                    mm = s.activation(attn_acc[:, l * DM + dch * 512: l * DM + dch * 512 + 512],
                                      c.bank(l, 0, 512), AF.Identity)
                    mk = c.inc(mm, "sS")
                    c.set_reader([l], mk)
                    dr_marks[(dch, l)] = mk
            block.scalar(p4_dr)

        ep_marks = {}
        x2T_marks = {}
        xm_prev = [None]
        for l in range(NT):
            m5 = {}
            base = 4 * l

            def ep_sync(q, l=l):
                c.wait(q, dr_marks[(3, l)])
                mm = q.dma_start(out=xst4[:, (l % 2) * DM:(l % 2) * DM + DM],
                                 in_=xseg[3 + 128 * l: 3 + 128 * (l + 1), :])
                m5["x"] = c.dinc(mm, f"dX4{l % 2}")
                if l >= 2:
                    c.wait(q, ep_marks[l - 2])
            block.sync(ep_sync)

            def ep_v(v, l=l, base=base):
                c.wait(v, m5["x"])
                xin = xst4[:, (l % 2) * DM:(l % 2) * DM + DM]
                v.tensor_scalar(ar[:, :], xin, statc[:, base:base + 1],
                                statc[:, base + 1:base + 2], ALU.subtract, ALU.mult)
                v.tensor_add(ar[:, :], ar[:, :], attn_acc[:, l * DM:(l + 1) * DM])
                ln_stats(v, ar[:, :], 128, DM)
                v.tensor_scalar(colw[:, 24:25], bnst[:, 49:50], EPS, None, ALU.add)
                mm = v.reciprocal(colw[:, 25:26], colw[:, 24:25])
                m5["st2"] = c.inc(mm, "sV")
            block.vector(ep_v)

            def ep_s(s, l=l):
                c.wait(s, m5["st2"])
                mm = s.activation(colw[:, 26:27], colw[:, 25:26], AF.Sqrt)
                m5["rv2"] = c.inc(mm, "sS")
            block.scalar(ep_s)

            def ep_v2(v, l=l):
                c.wait(v, m5["rv2"])
                if l >= 1:
                    c.wait(v, xm_prev[0])
                xin = xst4[:, (l % 2) * DM:(l % 2) * DM + DM]
                v.tensor_scalar(ar[:, :], ar[:, :], bnst[:, 48:49], colw[:, 26:27],
                                ALU.subtract, ALU.mult)
                v.tensor_add(xmd[:, :], xin, ar[:, :])
                ln_stats(v, xmd[:, :], 128, DM)
                v.tensor_scalar(colw[:, 27:28], bnst[:, 49:50], EPS, None, ALU.add)
                mm = v.reciprocal(colw[:, 28:29], colw[:, 27:28])
                m5["st3"] = c.inc(mm, "sV")
            block.vector(ep_v2)

            def ep_sync2(q, l=l):
                c.wait(q, m5["st3"])
                m5["xm"] = c.dinc(q.dma_start(out=xmid_dram[128 * l:128 * (l + 1), :],
                                              in_=xmd[:, :]), "dXM")
                xm_prev[0] = m5["xm"]
            block.sync(ep_sync2)

            def ep_s2(s, l=l):
                c.wait(s, m5["st3"])
                mm = s.activation(colw[:, 29:30], colw[:, 28:29], AF.Sqrt)
                m5["rv3"] = c.inc(mm, "sS")
            block.scalar(ep_s2)

            def ep_v3(v, l=l):
                c.wait(v, m5["rv3"])
                mm = v.tensor_scalar(x2b[:, :], xmd[:, :], bnst[:, 48:49], colw[:, 29:30],
                                     ALU.subtract, ALU.mult)
                m5["x2"] = c.inc(mm, "sV")
            block.vector(ep_v3)

            def ep_t(te, l=l):
                c.wait(te, m5["x2"])
                for k in range(KT):
                    b = (l * KT + k) % 4
                    off = ((l * KT + k) // 4) % 4 * 128
                    c.pre_write(te, [b])
                    mm = te.transpose(c.bank(b, off, 128), x2b[:, k * 128:(k + 1) * 128],
                                      ident[:, :])
                    c.set_writer([b], c.inc(mm, "sT"))
                    m5[("tr", k)] = c.cur("sT")
            block.tensor(ep_t)

            def ep_v4(v, l=l):
                c.wait(v, m5[("tr", KT - 1)])
                last = None
                for k in range(KT):
                    b = (l * KT + k) % 4
                    off = ((l * KT + k) // 4) % 4 * 128
                    mm = v.tensor_copy(x2T[:, k * 1024 + l * 128: k * 1024 + (l + 1) * 128],
                                       c.bank(b, off, 128))
                    last = c.inc(mm, "sV")
                c.set_reader([0, 1, 2, 3], last)
                ep_marks[l] = last
                x2T_marks[l] = last
            block.vector(ep_v4)
        m_x2T_done = ep_marks[NT - 1]

        if upto == "p4":
            _tail_stub(c, block, out, ident)
            stP4.close()
            return nc

        stP4.close()
        y2acc = sbL("y2acc", [128, NT * DM], F32)
        stP5 = ExitStack()
        hq = sbL("hq", [128, 4 * 1024], BF16, stP5)
        ha = sbL("ha", [128, 4 * 1024], BF16, stP5)
        xcmp = sbL("xcmp", [128, 3 * 1024], BF16, stP5)
        gsil = sbL("gsil", [128, 1024], F32, stP5)
        kbuf = sbL("kbuf", [128, 1024], BF16, stP5)
        wgst = sbL("wgst", [128, 2 * 8 * 128], F32R, stP5)
        wust = sbL("wust", [128, 2 * 8 * 128], F32R, stP5)
        wdst = sbL("wdst", [128, 2048], F32, stP5)
        wdb = sbL("wdb", [128, 4 * 2048], BF16, stP5)

        wg_r = wg_p.rearrange("(a p) e -> p a e", p=128)
        wu_r = wu_p.rearrange("(a p) e -> p a e", p=128)
        gu_rd = {}
        gw_half = {}
        gu_half = {}
        wd_rd = {}
        hq_free = {}
        down_t = {}
        last_h = [None]
        for qd in range(16):
            its = [qd, qd + 16, qd + 32, qd + 48]
            mq = {}
            for si, it in enumerate(its):
                def f_sync(q, qd=qd, si=si, it=it):
                    idx = qd * 4 + si
                    if idx < 2:
                        c.wait(q, m_x2T_done)
                        c.wait(q, xm_prev[0])
                    for hf2 in range(2):
                        sx = hf2
                        if idx >= 1:
                            c.wait(q, gu_half.get((idx - 1, hf2)))
                        mm = q.dma_start(out=wgst[:, sx * 8 * 128:(sx + 1) * 8 * 128]
                                         .rearrange("p (a e) -> p a e", a=8),
                                         in_=wg_r[:, hf2 * 8:(hf2 + 1) * 8, it * 128:(it + 1) * 128])
                        mq[("wg", si, hf2)] = c.dinc(mm, f"dWG{hf2}")
                        mm = q.dma_start(out=wust[:, sx * 8 * 128:(sx + 1) * 8 * 128]
                                         .rearrange("p (a e) -> p a e", a=8),
                                         in_=wu_r[:, hf2 * 8:(hf2 + 1) * 8, it * 128:(it + 1) * 128])
                        mq[("wu", si, hf2)] = c.dinc(mm, f"dWU{hf2}")
                    if idx >= 1:
                        c.wait(q, wd_rd.get(idx - 1))
                    mm = q.dma_start(out=wdst[:, :],
                                     in_=wd_p[it * 128:(it + 1) * 128, :])
                    mq[("wd", si)] = c.dinc(mm, "dWD")
                block.sync(f_sync)

                def f_g(g, qd=qd, si=si):
                    idx = qd * 4 + si
                    if idx == 0:
                        c.wait(g, m_x2T_done)
                        c.wait(g, xm_prev[0])
                    c.wait(g, mq[("wd", si)])
                    if qd >= 1:
                        c.wait(g, down_t.get((qd - 1, NT - 1), None))
                    mm = g.tensor_copy(wdb[:, si * 2048:(si + 1) * 2048], wdst[:, :])
                    wd_rd[idx] = c.inc(mm, "sG")
                    mq[("wdb", si)] = wd_rd[idx]
                block.gpsimd(f_g)

                def f_t(te, qd=qd, si=si):
                    idx = qd * 4 + si
                    if qd == 0 and si == 0:
                        c.wait(te, m_x2T_done)
                        c.wait(te, xm_prev[0])
                    if qd >= 1:
                        c.wait(te, hq_free.get(qd - 1))
                    c.pre_write(te, [4, 5])
                    for hf2 in range(2):
                        sx = hf2
                        c.wait(te, mq[("wg", si, hf2)])
                        for kk in range(8):
                            k = hf2 * 8 + kk
                            stat = wgst[:, sx * 8 * 128 + kk * 128: sx * 8 * 128 + (kk + 1) * 128]
                            for hx in range(2):
                                mm = te.matmul(c.bank(4 + hx, 0, 512), stat,
                                               x2T[:, k * 1024 + hx * 512: k * 1024 + (hx + 1) * 512],
                                               start=(k == 0), stop=(k == KT - 1))
                        gw_half[(idx, hf2)] = c.inc(mm, "sT")
                    mk = gw_half[(idx, 1)]
                    c.set_writer([4, 5], mk)
                block.tensor(f_t)

                def f_s(s, qd=qd, si=si):
                    if qd == 0 and si == 0:
                        c.wait(s, m_x2T_done)
                        c.wait(s, xm_prev[0])
                    if last_h[0] is not None:
                        c.wait(s, last_h[0])
                    c.pre_read(s, [4, 5])
                    mm = s.activation(gsil[:, :], c.ps[:, 4 * 512: 4 * 512 + 1024], AF.Silu)
                    mk = c.inc(mm, "sS")
                    c.set_reader([4, 5], mk)
                    mq[("gs", si)] = mk
                block.scalar(f_s)

                def f_t2(te, qd=qd, si=si):
                    idx = qd * 4 + si
                    c.wait(te, mq[("gs", si)])
                    c.pre_write(te, [6, 7])
                    for hf2 in range(2):
                        sx = hf2
                        c.wait(te, mq[("wu", si, hf2)])
                        for kk in range(8):
                            k = hf2 * 8 + kk
                            stat = wust[:, sx * 8 * 128 + kk * 128: sx * 8 * 128 + (kk + 1) * 128]
                            for hx in range(2):
                                mm = te.matmul(c.bank(6 + hx, 0, 512), stat,
                                               x2T[:, k * 1024 + hx * 512: k * 1024 + (hx + 1) * 512],
                                               start=(k == 0), stop=(k == KT - 1))
                        gu_half[(idx, hf2)] = c.inc(mm, "sT")
                    mk = gu_half[(idx, 1)]
                    c.set_writer([6, 7], mk)
                    gu_rd[idx] = mk
                block.tensor(f_t2)

                def f_v(v, qd=qd, si=si):
                    c.pre_read(v, [6, 7])
                    mm = v.tensor_tensor(hq[:, si * 1024:(si + 1) * 1024],
                                         c.ps[:, 6 * 512: 6 * 512 + 1024], gsil[:, :], ALU.mult)
                    mk = c.inc(mm, "sV")
                    c.set_reader([6, 7], mk)
                    mq[("h", si)] = mk
                    last_h[0] = mk
                block.vector(f_v)

            def sp_s(s, qd=qd):
                for si in range(4):
                    c.wait(s, mq[("h", si)])
                for si in range(4):
                    mm = s.activation(ha[:, si * 1024:(si + 1) * 1024],
                                      hq[:, si * 1024:(si + 1) * 1024], AF.Abs)
                mq["abs"] = c.inc(mm, "sS")
            block.scalar(sp_s)

            def sp_v(v, qd=qd):
                c.wait(v, mq["abs"])
                a = [ha[:, si * 1024:(si + 1) * 1024] for si in range(4)]
                x = [xcmp[:, i * 1024:(i + 1) * 1024] for i in range(3)]
                rk = sbtmp = None
                v.tensor_tensor(x[0], a[0], a[1], ALU.is_ge)
                v.tensor_tensor(x[1], a[0], a[2], ALU.is_ge)
                v.tensor_tensor(x[2], a[0], a[3], ALU.is_ge)
                v.tensor_add(a[0], x[0], x[1])
                v.tensor_add(a[0], a[0], x[2])
                v.tensor_scalar(x[0], x[0], -1.0, None, ALU.mult)
                v.tensor_scalar(x[1], x[1], -1.0, None, ALU.mult)
                v.tensor_scalar(x[2], x[2], -1.0, None, ALU.mult)
                v.tensor_tensor(kbuf[:, :], a[1], a[2], ALU.is_ge)
                v.scalar_tensor_tensor(x[1], kbuf[:, :], -1.0, x[1], ALU.mult, ALU.add)
                v.tensor_add(x[0], x[0], kbuf[:, :])
                v.tensor_tensor(kbuf[:, :], a[1], a[3], ALU.is_ge)
                v.tensor_add(x[0], x[0], kbuf[:, :])
                v.scalar_tensor_tensor(x[2], kbuf[:, :], -1.0, x[2], ALU.mult, ALU.add)
                v.tensor_tensor(kbuf[:, :], a[2], a[3], ALU.is_ge)
                v.tensor_add(x[1], x[1], kbuf[:, :])
                v.scalar_tensor_tensor(x[2], kbuf[:, :], -1.0, x[2], ALU.mult, ALU.add)
                ranks = [a[0], x[0], x[1], x[2]]
                thr = [1.5, 0.5, -0.5, -1.5]
                for si in range(4):
                    v.tensor_scalar(kbuf[:, :], ranks[si], thr[si], None, ALU.is_ge)
                    v.tensor_mul(hq[:, si * 1024:(si + 1) * 1024],
                                 hq[:, si * 1024:(si + 1) * 1024], kbuf[:, :])
                mq["sp"] = c.inc(v.tensor_copy(colw[:1, 30:31], ones[:1, 0:1]), "sV")
            block.vector(sp_v)

            for l in range(NT):
                def dn_t(te, qd=qd, l=l):
                    c.wait(te, mq["sp"])
                    for si in range(4):
                        c.wait(te, mq[("wdb", si)])
                    grp = [0, 1, 2, 3] if l % 2 == 0 else [4, 5, 6, 7]
                    if qd == 0 and l <= 1:
                        pass
                    c.pre_write(te, grp)
                    for si in range(4):
                        for dc in range(4):
                            mm = te.matmul(c.bank(grp[dc], 0, 512),
                                           hq[:, si * 1024 + l * 128: si * 1024 + (l + 1) * 128],
                                           wdb[:, si * 2048 + dc * 512: si * 2048 + (dc + 1) * 512],
                                           start=(si == 0), stop=(si == 3))
                    mk = c.inc(mm, "sT")
                    c.set_writer(grp, mk)
                    down_t[(qd, l)] = mk
                block.tensor(dn_t)

                def dn_v(v, qd=qd, l=l):
                    grp = [0, 1, 2, 3] if l % 2 == 0 else [4, 5, 6, 7]
                    c.pre_read(v, grp)
                    pslice = c.ps[:, grp[0] * 512: grp[0] * 512 + 2048]
                    dst = y2acc[:, l * DM:(l + 1) * DM]
                    if qd == 0:
                        mm = v.tensor_copy(dst, pslice)
                    else:
                        mm = v.tensor_add(dst, dst, pslice)
                    mk = c.inc(mm, "sV")
                    c.set_reader(grp, mk)
                    if l == NT - 1:
                        hq_free[qd] = mk
                block.vector(dn_v)
        m_ffn_done = c.cur("sV")

        out_marks = []
        for l in range(NT):
            m6 = {}

            def p6_sync(q, l=l):
                c.wait(q, m_ffn_done)
                if l >= 1:
                    c.wait(q, out_marks[l - 1])
                mm = q.dma_start(out=hq[:, :].bitcast(F32),
                                 in_=xmid_dram[128 * l:128 * (l + 1), :])
                m6["xm"] = c.dinc(mm, "dXR")
            block.sync(p6_sync)

            def p6_v(v, l=l):
                c.wait(v, m6["xm"])
                if l >= 2:
                    c.wait(v, out_marks[l - 2])
                ln_stats(v, y2acc[:, l * DM:(l + 1) * DM], 128, DM)
                v.tensor_scalar(colw[:, 24:25], bnst[:, 49:50], EPS, None, ALU.add)
                mm = v.reciprocal(colw[:, 25:26], colw[:, 24:25])
                m6["st"] = c.inc(mm, "sV")
            block.vector(p6_v)

            def p6_s(s, l=l):
                c.wait(s, m6["st"])
                mm = s.activation(colw[:, 26:27], colw[:, 25:26], AF.Sqrt)
                m6["rv"] = c.inc(mm, "sS")
            block.scalar(p6_s)

            def p6_v2(v, l=l):
                c.wait(v, m6["rv"])
                v.tensor_scalar(ha[:, :].bitcast(F32),
                                y2acc[:, l * DM:(l + 1) * DM], bnst[:, 48:49], colw[:, 26:27],
                                ALU.subtract, ALU.mult)
                mm = v.tensor_add(ha[:, :].bitcast(F32), ha[:, :].bitcast(F32),
                                  hq[:, :].bitcast(F32))
                m6["o"] = c.inc(mm, "sV")
            block.vector(p6_v2)

            def p6_sync2(q, l=l):
                c.wait(q, m6["o"])
                mm = q.dma_start(out=out[128 * l:128 * (l + 1), :],
                                 in_=ha[:, :].bitcast(F32))
                out_marks.append(c.dinc(mm, "dOUT"))
            block.sync(p6_sync2)

        def fin_sync(q):
            c.wait(q, out_marks[-1])
        block.sync(fin_sync)
        stP5.close()
        return nc


def _tail_stub(c, block, out, srcbuf):
    def syncf(q):
        c.wait(q, c.cur("sG"))
        c.wait(q, c.cur("sS"))
        c.wait(q, c.cur("sV"))
        c.wait(q, c.cur("sT"))
        c.wait(q, c.cur("sCC"))
        c.wait(q, c.cur("dD"))
        last = None
        for t in range(NT):
            last = c.dinc(q.dma_start(
                out=out[128 * t:128 * (t + 1), :].rearrange("p (a e) -> p a e", a=16),
                in_=srcbuf[:, 0:128].unsqueeze(1).broadcast_to([128, 16, 128])), "dOUT")
        c.wait(q, last)
    block.sync(syncf)


def host_layout(inputs, core):
    x = np.asarray(inputs["L0_attn_hidden_states"], np.float32).reshape(8192, 2048)
    r0 = core * T
    seg = np.zeros((3 + T, DM), np.float32)
    lo = max(0, r0 - 3)
    seg[3 - (r0 - lo):3] = x[lo:r0]
    seg[3:] = x[r0:r0 + T]
    win_t = np.zeros((DM, EIN + NH), np.float32)
    win_t[:, :EIN] = np.asarray(inputs["L0_attn_in_proj_w"], np.float32).T
    wout_t = np.ascontiguousarray(np.asarray(inputs["L0_attn_out_proj_w"], np.float32).T)
    perm = 4 * (np.arange(DFF) % 2048) + np.arange(DFF) // 2048
    wg_p = np.ascontiguousarray(np.asarray(inputs["L0_ffn_w_gate"], np.float32)[:, perm])
    wu_p = np.ascontiguousarray(np.asarray(inputs["L0_ffn_w_up"], np.float32)[:, perm])
    wd_p = np.ascontiguousarray(np.asarray(inputs["L0_ffn_w_down"], np.float32)[perm, :])
    cw = np.asarray(inputs["L0_attn_conv_w"], np.float32)
    cb = np.asarray(inputs["L0_attn_conv_b"], np.float32)
    convw = np.zeros((128, 34 * 4), np.float32)
    convb = np.zeros((128, 34), np.float32)
    for ct in range(32):
        convw[:, ct * 4:(ct + 1) * 4] = cw[ct * 128:(ct + 1) * 128]
        convb[:, ct] = cb[ct * 128:(ct + 1) * 128]
    convw[:64, 32 * 4:33 * 4] = cw[4096:4160]
    convb[:64, 32] = cb[4096:4160]
    convw[:64, 33 * 4:34 * 4] = cw[4160:4224]
    convb[:64, 33] = cb[4160:4224]
    rmsw_ = np.ascontiguousarray(np.asarray(inputs["L0_attn_rms_weight"], np.float32).reshape(KT, 128).T)
    normw_ = np.ascontiguousarray(np.asarray(inputs["L0_attn_norm_w"], np.float32).reshape(DI // 128, 128).T)
    return {
        "xseg": seg, "win_t": win_t, "wout_t": wout_t, "wg_p": wg_p, "wu_p": wu_p, "wd_p": wd_p,
        "convw": convw, "convb": convb, "rmsw": rmsw_, "normw": normw_,
        "dtb": np.asarray(inputs["L0_attn_dt_bias"], np.float32).reshape(NH, 1),
        "alog": np.asarray(inputs["L0_attn_A_log"], np.float32).reshape(NH, 1),
        "dvec": np.asarray(inputs["L0_attn_D"], np.float32).reshape(NH, 1),
        "cmask": (np.arange(NCORE) < core).astype(np.float32).reshape(1, NCORE),
    }


def kernel(**inputs):
    nc = build_program("full")
    in_maps = [host_layout(inputs, core) for core in range(NCORE)]
    res = run_bass_kernel_spmd(nc, in_maps, core_ids=list(range(NCORE)))
    outp = np.concatenate([r["out"] for r in res.results], axis=0)
    return outp.reshape(1, 8192, 2048).astype(np.float32)


# version 04
# speedup vs baseline: 1.0465x; 1.0465x over previous
import numpy as np
from contextlib import ExitStack

import concourse.bass as bass
import concourse.mybir as mybir
from concourse.bass_utils import run_bass_kernel_spmd

F32 = mybir.dt.float32
F32R = mybir.dt.float32r
BF16 = mybir.dt.bfloat16
AF = mybir.ActivationFunctionType
ALU = mybir.AluOpType

NCORE = 8
T = 1024
NT = 8
Q = 128
DM = 2048
DI = 4096
DS = 64
NH = 64
HD = 64
EIN = 8384
DFF = 8192
EPS = 1e-5
KT = 16
XCT = 32
W = 1028



_TRACKED = {
    "tensor_copy", "tensor_tensor", "tensor_add", "tensor_sub", "tensor_mul",
    "tensor_max", "tensor_scalar", "tensor_single_scalar", "tensor_scalar_add",
    "tensor_scalar_sub", "tensor_scalar_mul", "tensor_scalar_max", "tensor_scalar_min",
    "tensor_relu", "tensor_reduce", "reduce_max", "reduce_sum", "scalar_tensor_tensor",
    "tensor_tensor_scan", "reciprocal", "bn_stats", "bn_aggr", "select",
    "copy_predicated", "activation", "memset", "memzero", "iota", "affine_select",
    "add", "mul", "copy", "sqrt", "square", "sign",
}


class EngProxy:

    def __init__(self, eng, haz):
        self._e = eng
        self._h = haz

    def __getattr__(self, name):
        fn = getattr(self._e, name)
        if name not in _TRACKED:
            return fn

        def wrapped(*args, **kw):
            aps = [a for a in args if isinstance(a, bass.AP)]
            for k, v in kw.items():
                if isinstance(v, bass.AP):
                    aps.append(v) if k != "accum_out" else None
            outs = set()
            ins = set()
            if aps:
                outs.add(aps[0].tensor.name)
                for a in aps[1:]:
                    ins.add(a.tensor.name)
            ao = kw.get("accum_out")
            if isinstance(ao, bass.AP):
                outs.add(ao.tensor.name)
            h = self._h
            if (h["w"] & (ins | outs)) or (h["r"] & outs):
                self._e.drain()
                h["w"].clear()
                h["r"].clear()
            h["w"] |= outs
            h["r"] |= ins
            return fn(*args, **kw)
        return wrapped


class Ctx:
    def __init__(self, nc, st):
        self.nc = nc
        self._pool = [st.enter_context(nc.semaphore(f"sem{i}")) for i in range(56)]
        self._next = 0
        self.block = st.enter_context(nc.Block())
        self.sem = {}
        for name in ("sT", "sV", "sS", "sG", "sCC"):
            self._get(name)
        self.ps = st.enter_context(nc.psum_tensor("ps", [128, 4096], F32))
        self.bw = [None] * 8
        self.br = [None] * 8

    def _get(self, name):
        if name not in self.sem:
            assert self._next < len(self._pool), "semaphore pool exhausted"
            self.sem[name] = [self._pool[self._next], 0]
            self._next += 1
        return self.sem[name]

    def inc(self, instr, name, n=1):
        e = self._get(name)
        instr.then_inc(e[0], n)
        e[1] += n
        return (name, e[1])

    def dinc(self, instr, name):
        return self.inc(instr, name, 16)

    def wait(self, eng, mark):
        if mark is not None:
            eng.wait_ge(self.sem[mark[0]][0], mark[1])

    def cur(self, name):
        return (name, self._get(name)[1])

    def bank(self, b, off=0, width=512, parts=128, p0=0):
        return self.ps[p0:p0 + parts, b * 512 + off: b * 512 + off + width]

    def pre_write(self, eng, banks):
        for b in banks:
            self.wait(eng, self.br[b])

    def set_writer(self, banks, mark):
        for b in banks:
            self.bw[b] = mark

    def pre_read(self, eng, banks):
        for b in banks:
            self.wait(eng, self.bw[b])

    def set_reader(self, banks, mark):
        for b in banks:
            self.br[b] = mark


def build_program(upto="full", dump=()):
    nc = bass.Bass("TRN2", target_bir_lowering=False, debug=False)
    dump = set(dump)

    def dram(name, shape, dtype, kind=None):
        if kind is None:
            kind = "ExternalOutput" if name in dump else "Internal"
        if kind == "Internal":
            return nc.dram_tensor(name, shape, dtype).ap()
        return nc.dram_tensor(name, shape, dtype, kind=kind).ap()

    xseg = dram("xseg", [3 + T, DM], F32, kind="ExternalInput")
    win_t = dram("win_t", [DM, EIN + NH], F32R, kind="ExternalInput")
    win_x = dram("win_x", [35, 2, 128, 1024], F32R, kind="ExternalInput")
    wout_t = dram("wout_t", [4, 32, 128, 512], F32, kind="ExternalInput")
    wg_p = dram("wg_p", [64, 2, 128, 1024], F32R, kind="ExternalInput")
    wu_p = dram("wu_p", [64, 2, 128, 1024], F32R, kind="ExternalInput")
    wd_p = dram("wd_p", [DFF, DM], F32, kind="ExternalInput")
    convw = dram("convw", [128, 34 * 4], F32, kind="ExternalInput")
    convb = dram("convb", [128, 34], F32, kind="ExternalInput")
    rmsw = dram("rmsw", [128, KT], F32, kind="ExternalInput")
    normw = dram("normw", [128, DI // 128], F32, kind="ExternalInput")
    dtb = dram("dtb", [NH, 1], F32, kind="ExternalInput")
    alog = dram("alog", [NH, 1], F32, kind="ExternalInput")
    dvec = dram("dvec", [NH, 1], F32, kind="ExternalInput")
    cmask = dram("cmask", [1, NCORE], F32, kind="ExternalInput")
    out = dram("out", [T, DM], F32, kind="ExternalOutput")

    z_dram = dram("z_dram", [T, DI], BF16)
    xmid_dram = dram("xmid_dram", [T, DM], F32)
    cum_dram = dram("cum_dram", [NH, T], F32)
    ynorm_dram = dram("ynorm_dram", [XCT, 128, T], BF16)
    cc_in = dram("cc_in", [NH + 1, DI], F32)
    cc_out = dram("cc_out", [NCORE * (NH + 1), DI], F32)

    d_uT = dram("d_uT", [128, KT * W], F32) if "d_uT" in dump else None
    d_xbc = dram("d_xbc", [128, XCT * W], BF16) if "d_xbc" in dump else None
    d_bc = dram("d_bc", [64, 2 * W], BF16) if "d_bc" in dump else None
    d_dt = dram("d_dt", [NH, T], F32) if "d_dt" in dump else None
    d_cum = dram("d_cum", [NH, T], F32) if "d_cum" in dump else None
    d_sb = dram("d_sb", [NH, 2 * DI], F32) if "d_sb" in dump else None
    d_y = dram("d_y", [T, DI], F32) if "d_y" in dump else None

    st = ExitStack()
    with st:
        c = Ctx(nc, st)
        block = c.block
        _haz = {"v": {"w": set(), "r": set()}, "s": {"w": set(), "r": set()},
                "g": {"w": set(), "r": set()}}
        _bvec, _bsca, _bgps = block.vector, block.scalar, block.gpsimd
        block = type("B", (), {})()
        block.sync = c.block.sync
        block.tensor = c.block.tensor
        block.vector = lambda f: _bvec(lambda e: f(EngProxy(e, _haz["v"])))
        block.scalar = lambda f: _bsca(lambda e: f(EngProxy(e, _haz["s"])))
        block.gpsimd = lambda f: _bgps(lambda e: f(EngProxy(e, _haz["g"])))

        def sbL(name, shape, dt_, stk=st):
            return stk.enter_context(nc.sbuf_tensor(name, shape, dt_, side="left"))

        def sbR(name, shape, dt_, stk=st):
            return stk.enter_context(nc.sbuf_tensor(name, shape, dt_, side="right"))

        ident = sbL("ident", [128, 128], F32)
        identB = sbL("identB", [128, 128], BF16)
        ones = sbL("ones", [128, 128], F32)
        causal = sbL("causal", [128, 128], F32)
        cw_sb = sbL("cw_sb", [128, 34 * 4], F32)
        cb_sb = sbL("cb_sb", [128, 34], F32)
        rmsw_sb = sbL("rmsw_sb", [128, KT], F32)
        normw_sb = sbL("normw_sb", [128, DI // 128], F32)
        dtb_sb = sbL("dtb_sb", [NH, 1], F32)
        alog_sb = sbL("alog_sb", [NH, 1], F32)
        negA_sb = sbL("negA_sb", [NH, 1], F32)
        dmat = sbL("dmat", [128, NH], F32)
        cmrep = sbL("cmrep", [NH, NCORE], F32)
        statc = sbL("statc", [128, 4 * (NT + 1) + 4], F32)
        colw = sbL("colw", [128, 32], F32)
        bnst = sbL("bnst", [128, 52], F32)

        m0 = {}

        def p0_sync(q):
            last = None
            for dst, src in ((cw_sb, convw), (cb_sb, convb), (rmsw_sb, rmsw),
                             (normw_sb, normw), (dtb_sb, dtb), (alog_sb, alog)):
                last = c.dinc(q.dma_start(out=dst[:, :], in_=src[:, :]), "dP")
            last = c.dinc(q.dma_start(out=dmat[:, :],
                                      in_=dvec[:, 0].unsqueeze(0).broadcast_to([128, NH])), "dP")
            m0["params"] = c.dinc(q.dma_start(
                out=cmrep[:, :], in_=cmask[0, :].unsqueeze(0).broadcast_to([NH, NCORE])), "dP")
        block.sync(p0_sync)

        def p0_g(g):
            mk = c.inc(g.memset(ones[:, :], 1.0), "sG")
            c.wait(g, mk)
            c.inc(g.affine_select(ident[:, :], ones[:, :], [[0, 1], [0, 1], [0, 1], [-1, 128]],
                                  ALU.is_equal, 0.0, base=0, channel_multiplier=1), "sG")
            c.inc(g.affine_select(identB[:, :], ones[:, :], [[0, 1], [0, 1], [0, 1], [-1, 128]],
                                  ALU.is_equal, 0.0, base=0, channel_multiplier=1), "sG")
            m0["const"] = c.inc(g.affine_select(causal[:, :], ones[:, :],
                                                [[0, 1], [0, 1], [0, 1], [1, 128]],
                                                ALU.is_ge, 0.0, base=0, channel_multiplier=-1), "sG")
        block.gpsimd(p0_g)

        def p0_s(s):
            c.wait(s, m0["params"])
            m0["negA0"] = c.inc(s.activation(negA_sb[:, :], alog_sb[:, :], AF.Exp), "sS")
        block.scalar(p0_s)

        def p0_v(v):
            c.wait(v, m0["negA0"])
            m0["negA"] = c.inc(v.tensor_scalar_mul(negA_sb[:, :], negA_sb[:, :], -1.0), "sV")
        block.vector(p0_v)

        def ln_stats(v, src, P, width):
            nchunks = (width + 511) // 512
            for i in range(nchunks):
                w0 = min(512, width - i * 512)
                v.bn_stats(bnst[:P, i * 6:(i + 1) * 6], src[:, i * 512:i * 512 + w0])
            return v.bn_aggr(bnst[:P, 48:50], bnst[:P, 0:6 * nchunks])

        stB = ExitStack()
        uT = sbR("uT", [128, KT * W], F32R, stB)
        xst = sbR("xst", [128, 2 * DM], F32, stB)
        wstat = sbR("wstat", [128, 2 * 8 * 128], F32R, stB)
        stBq = ExitStack()
        dtsb = sbR("dtsb", [NH, T], F32, stBq)
        cum = sbR("cum", [NH, T], F32, stBq)
        lasb = sbR("lasb", [NH, T], F32, stBq)
        wloc = sbR("wloc", [NH, T], F32, stBq)
        acc = sbR("acc", [128, 1024], F32, stBq)

        stC = ExitStack()
        xbc = sbL("xbc", [128, XCT * W], BF16, stC)
        bbuf = sbL("bbuf", [64, W], BF16, stC)
        cbuf = sbL("cbuf", [64, W], BF16, stC)
        glog = sbL("glog", [NH, NT + 1], F32, stC)
        gcol = sbL("gcol", [NH, NT], F32, stC)
        ncumTr = sbL("ncumTr", [128, NT * NH], F32, stC)
        dtTr = sbL("dtTr", [128, NT * NH], F32, stC)
        wTr = sbL("wTr", [128, NT * NH], F32, stC)
        wgTr = sbL("wgTr", [128, NT * NH], F32, stC)
        btr = sbL("btr", [128, DS], BF16, stC)
        xseq = sbL("xseq", [128, DI], BF16, stC)
        xw = sbL("xw", [128, DI], BF16, stC)

        def pz_g(g):
            c.inc(g.memset(uT[:, :].bitcast(F32), 0.0), "sG")
            c.inc(g.memset(xbc[:, :], 0.0), "sG")
            c.inc(g.memset(bbuf[:, :], 0.0), "sG")
            m0["bufz"] = c.inc(g.memset(cbuf[:, :], 0.0), "sG")
        block.gpsimd(pz_g)

        uT_marks = []
        u_marks = {}
        for t in range(NT + 1):
            P = 128 if t < NT else 3
            par = t % 2
            base = 4 * t
            m1 = {}

            def p1_sync(q, t=t, par=par, P=P):
                if t >= 2:
                    c.wait(q, uT_marks[t - 2])
                src = xseg[3 + 128 * t: 3 + 128 * (t + 1), :] if t < NT else xseg[0:3, :]
                m1["x"] = c.dinc(q.dma_start(out=xst[:P, par * DM: par * DM + DM], in_=src), f"dX{t % 2}")
            block.sync(p1_sync)

            def p1_v(v, P=P, par=par, base=base, t=t):
                xin = xst[:P, par * DM: par * DM + DM]
                c.wait(v, m1["x"])
                if t >= 1:
                    c.wait(v, u_marks[t - 1])
                ln_stats(v, xin, P, DM)
                v.tensor_copy(statc[:P, base:base + 1], bnst[:P, 48:49])
                v.tensor_scalar(colw[:P, 2:3], bnst[:P, 49:50], EPS, None, ALU.add)
                v.reciprocal(colw[:P, 3:4], colw[:P, 2:3])
                v.tensor_mul(colw[:P, 4:5], bnst[:P, 49:50], colw[:P, 3:4])
                v.tensor_scalar(colw[:P, 4:5], colw[:P, 4:5], EPS, None, ALU.add)
                v.reciprocal(colw[:P, 5:6], colw[:P, 4:5])
                v.tensor_mul(colw[:P, 6:7], colw[:P, 3:4], colw[:P, 5:6])
                m1["cols"] = c.inc(v.tensor_copy(statc[:P, base + 1:base + 2], colw[:P, 3:4]), "sV")
            block.vector(p1_v)

            def p1_s(s, P=P, par=par, base=base, t=t):
                xin = xst[:P, par * DM: par * DM + DM]
                c.wait(s, m1["cols"])
                s.activation(statc[:P, base + 2:base + 3], colw[:P, 6:7], AF.Sqrt)
                s.activation(statc[:P, base + 1:base + 2], statc[:P, base + 1:base + 2], AF.Sqrt)
                s.activation(colw[:P, 7:8], statc[:P, base:base + 1], AF.Identity,
                             scale=statc[:P, base + 2:base + 3])
                s.activation(statc[:P, base + 3:base + 4], colw[:P, 7:8], AF.Identity, scale=-1.0)
                m1["u"] = c.inc(s.activation(xin, xin, AF.Identity,
                                             scale=statc[:P, base + 2:base + 3],
                                             bias=statc[:P, base + 3:base + 4]), "sS")
                u_marks[t] = m1["u"]
            block.scalar(p1_s)

            def p1_t(te, P=P, t=t, par=par):
                if t == 0:
                    c.wait(te, m0["const"])
                c.wait(te, m1["u"])
                xin = xst[:P, par * DM: par * DM + DM]
                for k in range(KT):
                    i = t * KT + k
                    b = i % 4
                    off = (i // 4) % 4 * 128
                    c.pre_write(te, [b])
                    mm = te.transpose(c.bank(b, off, P), xin[:, k * 128:(k + 1) * 128],
                                      ident[:P, :P])
                    c.set_writer([b], c.inc(mm, "sT"))
                    m1[("tr", k)] = c.cur("sT")
            block.tensor(p1_t)

            def p1_v2(v, P=P, t=t):
                if t == 0:
                    c.wait(v, m0["params"])
                    c.wait(v, m0["bufz"])
                c.wait(v, m1[("tr", KT - 1)])
                last = None
                for k in range(KT):
                    i = t * KT + k
                    b = i % 4
                    off = (i // 4) % 4 * 128
                    dstc = k * W + (128 * t if t < NT else 1024)
                    width = 128 if t < NT else 3
                    v.tensor_copy(uT[:, dstc:dstc + width], c.bank(b, off, width))
                    mm = v.tensor_scalar(uT[:, dstc:dstc + width], uT[:, dstc:dstc + width],
                                         rmsw_sb[:, k:k + 1], None, ALU.mult)
                    last = c.inc(mm, "sV")
                c.set_reader([0, 1, 2, 3], last)
                uT_marks.append(last)
            block.vector(p1_v2)
        m_uT_done = uT_marks[-1]
        if upto == "p1":
            _tail_stub(c, block, out, ident)
            stC.close()
            stBq.close()
            stB.close()
            return nc

        if "d_uT" in dump:
            def d1(q):
                c.wait(q, m_uT_done)
                c.dinc(q.dma_start(out=d_uT[:, :], in_=uT[:, :].bitcast(F32)), "dD")
            block.sync(d1)

        ets = [("x", j, 128, DI + 128 * j, 128, j) for j in range(XCT)]
        ets += [("b", 0, 64, 2 * DI, 128, 32), ("c", 0, 64, 2 * DI + 64, 128, 33),
                ("dt", 0, 64, 2 * DI + 128, 128, -1)]
        wr = win_t.rearrange("(a p) e -> p a e", p=128)
        tm_et = {}
        conv_mark = {}
        m_dt_ev = None
        wst_marks = {}
        for i, (kind, j, P, e0, ew, cwcol) in enumerate(ets):
            m2 = {}

            def p2_sync(q, i=i, e0=e0, ew=ew):
                for half in range(2):
                    hidx = 2 * i + half
                    par = hidx % 2
                    if hidx >= 2:
                        c.wait(q, tm_et[hidx - 2])
                    dst = wstat[:, par * 8 * 128: par * 8 * 128 + 8 * ew]
                    mm = q.dma_start(out=dst, in_=win_x[i, half])
                    wst_marks[hidx] = c.dinc(mm, f"dW{hidx % 2}")
            block.sync(p2_sync)

            def p2_t(te, i=i, ew=ew):
                if i == 0:
                    c.wait(te, m_uT_done)
                ebank = 2 * (i % 2)
                hbank = 4 + (i % 2)
                banks = [ebank, ebank + 1, hbank]
                c.pre_write(te, banks)
                for half in range(2):
                    hidx = 2 * i + half
                    par = hidx % 2
                    c.wait(te, wst_marks[hidx])
                    for kk in range(8):
                        k = half * 8 + kk
                        stat = wstat[:, par * 8 * 128 + kk * ew: par * 8 * 128 + (kk + 1) * ew]
                        first, last = (k == 0), (k == KT - 1)
                        te.matmul(c.bank(ebank, 0, 512, parts=ew), stat,
                                  uT[:, k * W: k * W + 512], start=first, stop=last)
                        te.matmul(c.bank(ebank + 1, 0, 512, parts=ew), stat,
                                  uT[:, k * W + 512: k * W + 1024], start=first, stop=last)
                        mm = te.matmul(c.bank(hbank, 0, 4, parts=ew), stat,
                                       uT[:, k * W + 1024: k * W + 1028], start=first, stop=last)
                    mk = c.inc(mm, "sT")
                    tm_et[hidx] = mk
                c.set_writer(banks, tm_et[2 * i + 1])
            block.tensor(p2_t)

            def p2_s(s, kind=kind, j=j, P=P, i=i):
                if i == 0:
                    c.wait(s, m0["bufz"])
                ebank = 2 * (i % 2)
                hbank = 4 + (i % 2)
                banks = [ebank, ebank + 1, hbank]
                c.pre_read(s, banks)
                if kind == "x":
                    dst = xbc[:, j * W: j * W + W]
                elif kind == "b":
                    dst = bbuf
                elif kind == "c":
                    dst = cbuf
                else:
                    dst = None
                if dst is not None:
                    s.activation(dst[:P, 3:3 + 1024], c.ps[:P, ebank * 512: ebank * 512 + 1024],
                                 AF.Identity)
                    mm = s.activation(dst[:P, 0:3], c.bank(hbank, 0, 3, parts=P), AF.Identity)
                else:
                    mm = s.activation(dtsb[:NH, 0:1024], c.ps[:NH, ebank * 512: ebank * 512 + 1024],
                                      AF.Identity)
                mk = c.inc(mm, "sS")
                c.set_reader(banks, mk)
                m2["ev"] = mk
            block.scalar(p2_s)
            if kind == "dt":
                m_dt_ev = m2["ev"]

            if kind != "dt":
                def p2_v(v, kind=kind, j=j, P=P, cwcol=cwcol, i=i):
                    buf = xbc[:, j * W: j * W + W] if kind == "x" else (bbuf if kind == "b" else cbuf)
                    if i == 0:
                        c.wait(v, m0["params"])
                    c.wait(v, m2["ev"])
                    if i >= 1:
                        c.wait(v, conv_mark[i - 1])
                    v.tensor_scalar(acc[:P, :], buf[:P, 0:1024],
                                    cw_sb[:P, cwcol * 4: cwcol * 4 + 1], None, ALU.mult)
                    for kk in range(1, 4):
                        v.scalar_tensor_tensor(acc[:P, :], buf[:P, kk:kk + 1024],
                                               cw_sb[:P, cwcol * 4 + kk: cwcol * 4 + kk + 1],
                                               acc[:P, :], ALU.mult, ALU.add)
                    m2["cv"] = c.inc(v.tensor_copy(colw[:P, 16:17], acc[:P, 0:1]), "sV")
                block.vector(p2_v)

                def p2_s2(s, kind=kind, j=j, P=P, cwcol=cwcol, i=i):
                    buf = xbc[:, j * W: j * W + W] if kind == "x" else (bbuf if kind == "b" else cbuf)
                    c.wait(s, m2["cv"])
                    mm = s.activation(buf[:P, 3:3 + 1024], acc[:P, :],
                                      AF.Silu, bias=cb_sb[:P, cwcol:cwcol + 1])
                    conv_mark[i] = c.inc(mm, "sS")
                block.scalar(p2_s2)
        m_conv_done = conv_mark[len(ets) - 2]

        if "d_xbc" in dump:
            def d2(q):
                c.wait(q, m_conv_done)
                c.dinc(q.dma_start(out=d_xbc[:, :], in_=xbc[:, :]), "dD")
                c.dinc(q.dma_start(out=d_bc[:, 0:W], in_=bbuf[:, :]), "dD")
                c.dinc(q.dma_start(out=d_bc[:, W:2 * W], in_=cbuf[:, :]), "dD")
            block.sync(d2)

        mdt = {}

        def pdt_v0(v):
            c.wait(v, m_dt_ev)
            mdt["dtb"] = c.inc(v.tensor_scalar(dtsb[:, :], dtsb[:, :], dtb_sb[:, :],
                                               None, ALU.add), "sV")
        block.vector(pdt_v0)

        def pdt_s(s):
            c.wait(s, mdt["dtb"])
            c.wait(s, m0["negA"])
            s.activation(lasb[:, :], dtsb[:, :], AF.Abs)
            s.activation(lasb[:, :], lasb[:, :], AF.Exp, scale=-1.0)
            mdt["a"] = c.inc(s.activation(cum[:, :], dtsb[:, :], AF.Relu), "sS")
        block.scalar(pdt_s)

        def pdt_v(v):
            c.wait(v, mdt["a"])
            mdt["b"] = c.inc(v.tensor_scalar(lasb[:, :], lasb[:, :], 1.0, None, ALU.add), "sV")
        block.vector(pdt_v)

        def pdt_s2(s):
            c.wait(s, mdt["b"])
            mdt["c"] = c.inc(s.activation(lasb[:, :], lasb[:, :], AF.Ln), "sS")
        block.scalar(pdt_s2)

        def pdt_v2(v):
            c.wait(v, mdt["c"])
            v.tensor_add(dtsb[:, :], lasb[:, :], cum[:, :])
            v.tensor_scalar(lasb[:, :], dtsb[:, :], negA_sb[:, :], None, ALU.mult)
            for ch in range(NT):
                v.tensor_tensor_scan(cum[:, ch * Q:(ch + 1) * Q], ones[:NH, :],
                                     lasb[:, ch * Q:(ch + 1) * Q], 0.0, ALU.mult, ALU.add)
            v.tensor_scalar(glog[:, 0:1], ones[:NH, 0:1], 0.0, None, ALU.mult)
            for ch in range(NT):
                v.tensor_add(glog[:, ch + 1:ch + 2], glog[:, ch:ch + 1],
                             cum[:, ch * Q + Q - 1: ch * Q + Q])
            for ch in range(NT):
                v.tensor_sub(gcol[:, ch:ch + 1], glog[:, NT:NT + 1], glog[:, ch:ch + 1])
            mdt["cum"] = c.inc(v.tensor_copy(colw[:NH, 17:18], glog[:, NT:NT + 1]), "sV")
        block.vector(pdt_v2)

        def pdt_sync(q):
            c.wait(q, mdt["cum"])
            mdt["cumd"] = c.dinc(q.dma_start(out=cum_dram[:, :], in_=cum[:, :]), "dCM")
        block.sync(pdt_sync)

        def pdt_s3(s):
            c.wait(s, mdt["cum"])
            for ch in range(NT):
                cs = slice(ch * Q, (ch + 1) * Q)
                s.activation(wloc[:, cs], cum[:, cs], AF.Exp, scale=-1.0,
                             bias=cum[:, ch * Q + Q - 1: ch * Q + Q])
                s.activation(lasb[:, cs], cum[:, cs], AF.Exp, scale=-1.0,
                             bias=gcol[:, ch:ch + 1])
            mdt["wx"] = c.inc(s.activation(colw[:1, 18:19], ones[:1, 0:1], AF.Identity), "sS")
        block.scalar(pdt_s3)

        def pdt_v3(v):
            c.wait(v, mdt["wx"])
            v.tensor_mul(wloc[:, :], wloc[:, :], dtsb[:, :])
            mdt["w"] = c.inc(v.tensor_mul(lasb[:, :], lasb[:, :], dtsb[:, :]), "sV")
        block.vector(pdt_v3)

        for ch in range(NT):
            mtr = {}

            def ptr_t(te, ch=ch):
                c.wait(te, mdt["w"])
                c.pre_write(te, [6])
                cs = slice(ch * Q, (ch + 1) * Q)
                te.transpose(c.bank(6, 0, NH), cum[:, cs], ident[:NH, :NH])
                te.transpose(c.bank(6, 64, NH), dtsb[:, cs], ident[:NH, :NH])
                te.transpose(c.bank(6, 128, NH), wloc[:, cs], ident[:NH, :NH])
                mm = te.transpose(c.bank(6, 192, NH), lasb[:, cs], ident[:NH, :NH])
                mk = c.inc(mm, "sT")
                c.set_writer([6], mk)
                mtr["t"] = mk
            block.tensor(ptr_t)

            def ptr_v(v, ch=ch):
                c.pre_read(v, [6])
                v.tensor_scalar(ncumTr[:, ch * NH:(ch + 1) * NH], c.bank(6, 0, NH), -1.0, None, ALU.mult)
                v.tensor_copy(dtTr[:, ch * NH:(ch + 1) * NH], c.bank(6, 64, NH))
                v.tensor_copy(wTr[:, ch * NH:(ch + 1) * NH], c.bank(6, 128, NH))
                mm = v.tensor_copy(wgTr[:, ch * NH:(ch + 1) * NH], c.bank(6, 192, NH))
                mk = c.inc(mm, "sV")
                c.set_reader([6], mk)
            block.vector(ptr_v)
        m_tr_done = c.cur("sV")
        if upto == "p2a":
            _tail_stub(c, block, out, ident)
            stC.close()
            stBq.close()
            stB.close()
            return nc
        stBq.close()

        if "d_dt" in dump:
            def d3(q):
                c.wait(q, mdt["cum"])
                c.dinc(q.dma_start(out=d_dt[:, :], in_=dtsb[:, :]), "dD")
                c.dinc(q.dma_start(out=d_cum[:, :], in_=cum[:, :]), "dD")
            block.sync(d3)

        stL8 = ExitStack()
        l8st = sbL("l8st", [NH, DI], F32, stL8)

        def xtranspose(ch, dst_bank, xw_src):
            xwm = {}

            def xt_t(te):
                c.pre_write(te, [dst_bank])
                mm = te.transpose(c.bank(dst_bank, 0, 32).bitcast(BF16),
                                  bbuf[:, 3 + ch * Q: 3 + (ch + 1) * Q], identB[:DS, :DS])
                c.set_writer([dst_bank], c.inc(mm, "sT"))
            block.tensor(xt_t)

            def xt_v(v):
                c.pre_read(v, [dst_bank])
                mm = v.tensor_copy(btr[:, :], c.bank(dst_bank, 0, 32).bitcast(BF16))
                mk = c.inc(mm, "sV")
                c.set_reader([dst_bank], mk)
            block.vector(xt_v)

            for grp in range(4):
                def xg_t(te, grp=grp):
                    c.pre_write(te, [dst_bank])
                    for u8 in range(8):
                        jj = grp * 8 + u8
                        mm = te.transpose(c.bank(dst_bank, u8 * 64, 64).bitcast(BF16),
                                          xbc[:, jj * W + 3 + ch * Q: jj * W + 3 + (ch + 1) * Q],
                                          identB[:, :])
                    c.set_writer([dst_bank], c.inc(mm, "sT"))
                block.tensor(xg_t)

                def xg_v(v, grp=grp):
                    c.pre_read(v, [dst_bank])
                    v.tensor_copy(xseq[:, grp * 1024:(grp + 1) * 1024],
                                  c.bank(dst_bank, 0, 512).bitcast(BF16))
                    for h in range(grp * 16, (grp + 1) * 16):
                        mm = v.tensor_scalar(xw[:, h * HD:(h + 1) * HD], xseq[:, h * HD:(h + 1) * HD],
                                             xw_src[:, ch * NH + h: ch * NH + h + 1], None, ALU.mult)
                    mk = c.inc(mm, "sV")
                    c.set_reader([dst_bank], mk)
                    xwm[grp] = mk
                block.vector(xg_v)
            return xwm

        for ch in range(NT):
            if ch == 0:
                def c0_t(te):
                    c.wait(te, m_conv_done)
                    c.wait(te, m_tr_done)
                block.tensor(c0_t)
            xwm = xtranspose(ch, 6, wgTr)
            for qt in range(4):
                def st_t(te, ch=ch, qt=qt):
                    c.wait(te, xwm[qt])
                    c.pre_write(te, [7])
                    for i2 in range(16):
                        h = qt * 16 + i2
                        mm = te.matmul(c.bank(7, i2 * 32, 64, parts=64).bitcast(F32)
                                       if False else c.bank(7, (i2 % 8) * 64, 64, parts=64),
                                       btr[:, :], xw[:, h * HD:(h + 1) * HD],
                                       start=(i2 % 8 == 0 and True), stop=True) \
                            if False else None
                    c.set_writer([7], None)
                block.tensor(st_t) if False else None

                for sq in range(2):
                    def st_t2(te, ch=ch, qt=qt, sq=sq):
                        c.wait(te, xwm[qt])
                        c.pre_write(te, [7])
                        for i2 in range(8):
                            h = qt * 16 + sq * 8 + i2
                            mm = te.matmul(c.bank(7, i2 * 64, 64, parts=64), btr[:, :],
                                           xw[:, h * HD:(h + 1) * HD],
                                           start=(i2 == 0), stop=(i2 == 7))
                        c.set_writer([7], c.inc(mm, "sT"))
                    block.tensor(st_t2)

                    def st_v2(v, ch=ch, qt=qt, sq=sq):
                        if ch == 0 and qt == 0 and sq == 0:
                            c.wait(v, mdt["cumd"])
                            if dump:
                                c.wait(v, c.cur("dD"))
                        c.pre_read(v, [7])
                        dst = l8st[:, (qt * 2 + sq) * 512:(qt * 2 + sq + 1) * 512]
                        if ch == 0:
                            mm = v.tensor_copy(dst, c.bank(7, 0, 512, parts=64))
                        else:
                            mm = v.tensor_add(dst, dst, c.bank(7, 0, 512, parts=64))
                        mk = c.inc(mm, "sV")
                        c.set_reader([7], mk)
                    block.vector(st_v2)
        mdt["l8"] = c.cur("sV")

        def cc_sync(q):
            c.wait(q, mdt["l8"])
            c.dinc(q.dma_start(out=cc_in[0:NH, :], in_=l8st[:, :]), "dCI")
            c.dinc(q.dma_start(out=cc_in[NH:NH + 1, NH:].rearrange("a (b e) -> a b e", b=63),
                               in_=ones[0:1, 0:NH].unsqueeze(1).broadcast_to([1, 63, NH])), "dCI")
            mdt["ccin"] = c.dinc(q.dma_start(out=cc_in[NH:NH + 1, 0:NH], in_=glog[:, NT:NT + 1]), "dCI")
            if d_sb is not None:
                c.dinc(q.dma_start(out=d_sb[:, 0:DI], in_=l8st[:, :]), "dD")
        block.sync(cc_sync)

        if upto == "p2s":
            _tail_stub(c, block, out, ident)
            stL8.close()
            stC.close()
            stB.close()
            return nc

        def cc_g(g):
            c.wait(g, mdt["ccin"])
            mm = g.collective_compute("AllGather", ALU.bypass,
                                      replica_groups=[list(range(NCORE))],
                                      ins=[cc_in[:, :].opt()], outs=[cc_out[:, :].opt()])
            mdt["cc"] = c.inc(mm, "sCC")
        block.gpsimd(cc_g)
        stL8.close()

        zW = sbR("zW", [128, 2 * 512], F32R, stB)
        zstg = sbR("zstg", [128, 4 * 512], BF16, stB)
        stg_rd = {}
        m_z_done = None
        zdma_prev = None
        for zc in range(NT):
            mz = {}
            for k in range(KT):
                par = k % 2
                mzk = {}

                def pz_sync(q, zc=zc, k=k, par=par):
                    if (zc * KT + k) < 2:
                        c.wait(q, m_tr_done)
                        c.wait(q, mdt["cumd"])
                        c.wait(q, mdt["ccin"])
                        if dump:
                            c.wait(q, c.cur("dD"))
                    else:
                        c.wait(q, stg_rd[zc * KT + k - 2])
                    mm = q.dma_start(out=zW[:, par * 512:(par + 1) * 512],
                                     in_=win_t[:, zc * 512:(zc + 1) * 512]
                                     .rearrange("(a p) e -> p a e", p=128)[:, k, :])
                    mzk["w"] = c.dinc(mm, f"dZ{k % 2}")
                (block.sync if k % 2 == 0 else block.scalar)(pz_sync)

                def pz_t(te, zc=zc, k=k, par=par):
                    c.wait(te, mzk["w"])
                    if k == 0:
                        c.pre_write(te, list(range(8)))
                    for l in range(NT):
                        mm = te.matmul(c.bank(l, 0, 512),
                                       uT[:, k * W + l * 128: k * W + l * 128 + 128],
                                       zW[:, par * 512:(par + 1) * 512],
                                       start=(k == 0), stop=(k == KT - 1))
                    mk = c.inc(mm, "sT")
                    stg_rd[zc * KT + k] = mk
                    if k == KT - 1:
                        c.set_writer(list(range(8)), mk)
                block.tensor(pz_t)

            for hf in range(2):
                def pz_s(s, zc=zc, hf=hf):
                    if zc == 0 and hf == 0:
                        c.wait(s, mdt["ccin"])
                    if zdma_prev is not None:
                        c.wait(s, zdma_prev)
                    c.pre_read(s, [4 * hf + i for i in range(4)])
                    for l in range(4):
                        mm = s.activation(zstg[:, l * 512:(l + 1) * 512],
                                          c.bank(4 * hf + l, 0, 512), AF.Identity)
                    mk = c.inc(mm, "sS")
                    c.set_reader([4 * hf + i for i in range(4)], mk)
                    mz[("dr", hf)] = mk
                block.scalar(pz_s)

                def pz_sync2(q, zc=zc, hf=hf):
                    c.wait(q, mz[("dr", hf)])
                    mm = q.dma_start(out=z_dram[hf * 512:(hf + 1) * 512, zc * 512:(zc + 1) * 512]
                                     .rearrange("(a p) e -> p a e", p=128),
                                     in_=zstg[:, :].rearrange("p (a e) -> p a e", a=4))
                    mz[("zd", hf)] = c.dinc(mm, "dZO")
                block.sync(pz_sync2)
                zdma_prev = mz[("zd", hf)]
            m_z_done = zdma_prev

        if upto == "p2":
            _tail_stub(c, block, out, ident)
            stC.close()
            stB.close()
            return nc
        stB.close()

        sbefore = sbL("sbefore", [NH, DI], F32, stC)
        sb16 = sbL("sb16", [NH, DI], BF16, stC)
        ed16 = sbL("ed16", [128, 2 * 16 * 128], F32, stC)
        stEC = ExitStack()
        eseg = sbL("eseg", [NH, 2 * DI], F32, stEC)
        almat = sbL("almat", [NH, NCORE * NH], F32, stEC)
        msum = sbL("msum", [NH, NH], F32, stEC)
        wexp = sbL("wexp", [NH, NH], F32, stEC)

        def ec_sync(q):
            c.wait(q, mdt["cc"])
            c.wait(q, m_z_done)
            src = cc_out.rearrange("(a p) e -> a p e", p=NH + 1)[:, NH, 0:NH]
            mdt["al"] = c.dinc(q.dma_start(out=almat[:, :].rearrange("p (a e) -> p a e", a=NCORE),
                                           in_=src.unsqueeze(0).broadcast_to([NH, NCORE, NH])), "dAL")
        block.sync(ec_sync)

        def ec_v0(v):
            c.wait(v, m_z_done)
            c.wait(v, mdt["al"])
            mdt[("ms", NCORE)] = c.inc(v.tensor_scalar(msum[:, :], almat[:, 0:NH], 0.0,
                                                       None, ALU.mult), "sV")
        block.vector(ec_v0)

        eseg_marks = {}
        for j in range(NCORE - 1, -1, -1):
            mej = {}

            def ej_sync(q, j=j):
                if j <= NCORE - 3:
                    c.wait(q, eseg_marks[j + 2])
                c.wait(q, mdt["cc"])
                c.wait(q, m_z_done)
                mm = q.dma_start(out=eseg[:, (j % 2) * DI:(j % 2) * DI + DI],
                                 in_=cc_out[(NH + 1) * j:(NH + 1) * j + NH, :])
                mej["sg"] = c.dinc(mm, f"dES{j % 2}")
            block.sync(ej_sync)

            def ej_s(s, j=j):
                c.wait(s, mdt[("ms", j + 1)])
                mej["we"] = c.inc(s.activation(wexp[:, :], msum[:, :], AF.Exp), "sS")
            block.scalar(ej_s)

            def ej_v(v, j=j):
                if j == NCORE - 1:
                    c.wait(v, m_z_done)
                c.wait(v, mej["we"])
                c.wait(v, mej["sg"])
                v.tensor_scalar(wexp[:, :], wexp[:, :], cmrep[:, j:j + 1], None, ALU.mult)
                es = eseg[:, (j % 2) * DI:(j % 2) * DI + DI]
                e3 = es.rearrange("p (a e) -> p a e", a=NH)
                w3 = wexp[:, :].unsqueeze(2).broadcast_to([NH, NH, HD])
                if j == NCORE - 1:
                    v.tensor_tensor(sbefore[:, :].rearrange("p (a e) -> p a e", a=NH), e3, w3, ALU.mult)
                else:
                    v.tensor_tensor(ed16[:NH, 0:DI].rearrange("p (a e) -> p a e", a=NH), e3, w3, ALU.mult)
                    v.tensor_add(sbefore[:, :], sbefore[:, :], ed16[:NH, 0:DI])
                mm = v.scalar_tensor_tensor(msum[:, :], almat[:, j * NH:(j + 1) * NH],
                                            cmrep[:, j:j + 1], msum[:, :], ALU.mult, ALU.add)
                mk = c.inc(mm, "sV")
                mdt[("ms", j)] = mk
                eseg_marks[j] = mk
            block.vector(ej_v)

        m_E = c.cur("sV")

        def sb16_v(v):
            mdt["sb16"] = c.inc(v.tensor_copy(sb16[:, :], sbefore[:, :]), "sV")
        block.vector(sb16_v)
        stEC.close()

        if "d_sb" in dump:
            def d4(q):
                c.wait(q, mdt["sb16"])
                c.dinc(q.dma_start(out=d_sb[:, DI:2 * DI], in_=sbefore[:, :]), "dD")
            block.sync(d4)

        if upto == "p2c":
            _tail_stub(c, block, out, ident)
            stC.close()
            return nc

        stP3 = ExitStack()
        mtb = sbL("mtb", [128, 2 * 16 * 128], BF16, stP3)
        bcq = sbL("bcq", [128, 2 * 2048], F32, stP3)
        yall = sbL("yall", [128, DI], F32, stP3)
        tv = sbL("tv", [128, 512], F32, stP3)
        zsl = sbL("zsl", [128, DI], BF16, stP3)
        ynstg = sbL("ynstg", [128, XCT * 128], BF16, stP3)
        gmb = sbL("gmb", [128, 128], F32, stP3)
        eccTr = sbL("eccTr", [128, NT * NH], F32, stP3)
        amat = sbL("amat", [NH, NT * NH], F32, stP3)

        def p3_prep_s(s):
            c.wait(s, mdt["cumd"])
            for ch in range(NT):
                s.activation(eccTr[:, ch * NH:(ch + 1) * NH], ncumTr[:, ch * NH:(ch + 1) * NH],
                             AF.Exp, scale=-1.0)
            mdt["p3prep"] = c.inc(s.activation(colw[:1, 18:19], ones[:1, 0:1], AF.Identity), "sS")
        block.scalar(p3_prep_s)

        gseq = [(ch, g) for ch in range(NT) for g in range(4)]
        bcq_marks = {}
        bcqm_marks = {}
        sed_marks = {}

        def bcq_dma(q, gi):
            ch, g = gseq[gi]
            if gi >= 2:
                c.wait(q, sed_marks[gi - 2])
            src = cum_dram[g * 16:(g + 1) * 16, ch * Q:(ch + 1) * Q]
            mm = q.dma_start(out=bcq[:, (gi % 2) * 2048:(gi % 2) * 2048 + 2048]
                             .rearrange("p (a e) -> p a e", a=16),
                             in_=src.unsqueeze(0).broadcast_to([128, 16, Q]))
            bcq_marks[gi] = c.dinc(mm, f"dB{gi % 2}")

        def prime_sync(q):
            c.wait(q, mdt["sb16"])
            bcq_dma(q, 0)
            bcq_dma(q, 1)
        block.sync(prime_sync)

        zsl_marks = {}
        yn_marks = {}
        dy_marks = {}
        mtb_marks = {}
        ysg_t = {}
        ysg_v = {}
        for ch in range(NT):
            mc3 = {}

            def p3_zsync(q, ch=ch):
                if ch >= 1:
                    c.wait(q, yn_marks[ch - 1])
                c.wait(q, m_z_done)
                mm = q.dma_start(out=zsl[:, :], in_=z_dram[ch * 128:(ch + 1) * 128, :])
                zsl_marks[ch] = c.dinc(mm, "dZS")
            block.sync(p3_zsync)

            def p3_g(te, ch=ch):
                if ch == 0:
                    c.wait(te, mdt["p3prep"])
                c.pre_write(te, [2])
                mm = te.matmul(c.bank(2, 0, 128), bbuf[:, 3 + ch * Q: 3 + (ch + 1) * Q],
                               cbuf[:, 3 + ch * Q: 3 + (ch + 1) * Q], start=True, stop=True)
                c.set_writer([2], c.inc(mm, "sT"))
            block.tensor(p3_g)

            def p3_gv(v, ch=ch):
                if ch == 0:
                    c.wait(v, m0["const"])
                c.pre_read(v, [2])
                mm = v.tensor_tensor(gmb[:, :], c.bank(2, 0, 128), causal[:, :], ALU.mult)
                mk = c.inc(mm, "sV")
                c.set_reader([2], mk)
                mc3["gm"] = mk
            block.vector(p3_gv)

            xwm = xtranspose(ch, 3, wTr)

            for g in range(4):
                gi = ch * 4 + g


                def pg_gm(g_, ch=ch, g=g, gi=gi):
                    c.wait(g_, bcq_marks[gi])
                    if gi >= 2:
                        c.wait(g_, sed_marks[gi - 2])
                    mm = g_.affine_select(bcq[:, (gi % 2) * 2048:(gi % 2) * 2048 + 2048],
                                          bcq[:, (gi % 2) * 2048:(gi % 2) * 2048 + 2048],
                                          [[0, 1], [0, 1], [0, 16], [1, 128]],
                                          ALU.is_ge, -1e30, base=0, channel_multiplier=-1)
                    bcqm_marks[gi] = c.inc(mm, "sG")
                block.gpsimd(pg_gm)

                def pg_s(s, ch=ch, g=g, gi=gi):
                    c.wait(s, bcqm_marks[gi])
                    if gi >= 2:
                        c.wait(s, mtb_marks[gi - 2])
                    for i2 in range(16):
                        h = g * 16 + i2
                        s.activation(ed16[:, (gi % 2) * 2048 + i2 * 128:
                                          (gi % 2) * 2048 + (i2 + 1) * 128],
                                     bcq[:, (gi % 2) * 2048 + i2 * 128:
                                         (gi % 2) * 2048 + (i2 + 1) * 128],
                                     AF.Exp, bias=ncumTr[:, ch * NH + h: ch * NH + h + 1])
                    acol = bcq[0:NH, (gi % 2) * 2048:(gi % 2) * 2048 + 2048] \
                        .rearrange("p (a e) -> p a e", e=128)[:, :, 127]
                    mm = s.activation(amat[:, ch * NH + g * 16: ch * NH + (g + 1) * 16],
                                      acol, AF.Exp)
                    sed_marks[gi] = c.inc(mm, "sS")
                block.scalar(pg_s)

                def pg_sync(q, gi=gi):
                    if gi + 2 < len(gseq):
                        bcq_dma(q, gi + 2)
                block.sync(pg_sync)

                def pg_v(v, ch=ch, g=g, gi=gi):
                    c.wait(v, sed_marks[gi])
                    c.wait(v, mc3["gm"])
                    if gi >= 2:
                        c.wait(v, ysg_t[(gi - 2, 1)])
                    for i2 in range(16):
                        h = g * 16 + i2
                        mm = v.scalar_tensor_tensor(
                            mtb[:, (gi % 2) * 2048 + i2 * 128:(gi % 2) * 2048 + (i2 + 1) * 128],
                            gmb[:, :], dtTr[:, ch * NH + h: ch * NH + h + 1],
                            ed16[:, (gi % 2) * 2048 + i2 * 128:(gi % 2) * 2048 + (i2 + 1) * 128],
                            ALU.mult, ALU.mult)
                    mtb_marks[gi] = c.inc(mm, "sV")
                block.vector(pg_v)

                for sg in range(2):
                    sgi = gi * 2 + sg

                    def pg_t(te, ch=ch, g=g, gi=gi, sg=sg, sgi=sgi):
                        c.wait(te, mtb_marks[gi])
                        c.wait(te, xwm[g])
                        if ch == 0 and g == 0 and sg == 0:
                            c.wait(te, mdt["sb16"])
                        ya, yi = (4, 5) if sgi % 2 == 0 else (6, 7)
                        c.pre_write(te, [ya, yi])
                        for i2 in range(8):
                            h = g * 16 + sg * 8 + i2
                            te.matmul(c.bank(ya, i2 * 64, 64),
                                      mtb[:, (gi % 2) * 2048 + (sg * 8 + i2) * 128:
                                          (gi % 2) * 2048 + (sg * 8 + i2 + 1) * 128],
                                      xseq[:, h * HD:(h + 1) * HD], start=(i2 == 0), stop=(i2 == 7))
                        for i2 in range(8):
                            h = g * 16 + sg * 8 + i2
                            mm = te.matmul(c.bank(yi, i2 * 64, 64),
                                           cbuf[:, 3 + ch * Q: 3 + (ch + 1) * Q],
                                           sb16[:, h * HD:(h + 1) * HD], start=(i2 == 0), stop=(i2 == 7))
                        mk = c.inc(mm, "sT")
                        c.set_writer([ya, yi], mk)
                        ysg_t[(gi, sg)] = mk
                    block.tensor(pg_t)

                    def pg_ev(v, ch=ch, g=g, gi=gi, sg=sg, sgi=sgi):
                        if dump and g == 0 and sg == 0 and ch >= 1:
                            c.wait(v, dy_marks.get(ch - 1))
                        ya, yi = (4, 5) if sgi % 2 == 0 else (6, 7)
                        c.pre_read(v, [ya, yi])
                        h0 = g * 16 + sg * 8
                        ecc = eccTr[:, ch * NH + h0: ch * NH + h0 + 8]
                        v.tensor_tensor(tv[:, :].rearrange("p (a e) -> p a e", a=8),
                                        c.bank(yi, 0, 512).rearrange("p (a e) -> p a e", a=8),
                                        ecc.unsqueeze(2).broadcast_to([128, 8, HD]), ALU.mult)
                        mm = v.tensor_add(yall[:, h0 * HD: h0 * HD + 512],
                                          c.bank(ya, 0, 512), tv[:, :])
                        mk = c.inc(mm, "sV")
                        c.set_reader([ya, yi], mk)
                        ysg_v[(gi, sg)] = mk
                    block.vector(pg_ev)

            for sq in range(8):
                def sc_t(te, ch=ch, sq=sq):
                    bb = sq % 2
                    c.wait(te, xwm[sq // 2])
                    c.pre_write(te, [bb])
                    for i2 in range(8):
                        h = sq * 8 + i2
                        mm = te.matmul(c.bank(bb, i2 * 64, 64, parts=64), btr[:, :],
                                       xw[:, h * HD:(h + 1) * HD], start=(i2 == 0), stop=(i2 == 7))
                    c.set_writer([bb], c.inc(mm, "sT"))
                block.tensor(sc_t)

                def sc_v(v, ch=ch, sq=sq):
                    bb = sq % 2
                    if dump and ch == 0 and sq == 0:
                        c.wait(v, c.cur("dD"))
                    c.wait(v, ysg_t[(ch * 4 + 3, 1)])
                    c.pre_read(v, [bb])
                    sl = sbefore[:, sq * 512:(sq + 1) * 512]
                    am = amat[:, ch * NH + sq * 8: ch * NH + sq * 8 + 8]
                    v.tensor_tensor(tv[:NH, :].rearrange("p (a e) -> p a e", a=8),
                                    sl.rearrange("p (a e) -> p a e", a=8),
                                    am.unsqueeze(2).broadcast_to([NH, 8, HD]), ALU.mult)
                    mm = v.tensor_add(sl, tv[:NH, :], c.bank(bb, 0, 512, parts=64))
                    mk = c.inc(mm, "sV")
                    c.set_reader([bb], mk)
                block.vector(sc_v)

            def sb16_u(v, ch=ch):
                mm = v.tensor_copy(sb16[:, :], sbefore[:, :])
                c.inc(mm, "sV")
            block.vector(sb16_u)

            def tail_v(v, ch=ch):
                c.wait(v, zsl_marks[ch])
                v.tensor_tensor(xw[:, :].rearrange("p (a e) -> p a e", a=NH),
                                xseq[:, :].rearrange("p (a e) -> p a e", a=NH),
                                dmat[:, :].unsqueeze(2).broadcast_to([128, NH, HD]), ALU.mult)
                mc3["dx"] = c.inc(v.tensor_add(yall[:, :], yall[:, :], xw[:, :]), "sV")
            block.vector(tail_v)

            def tail_s(s, ch=ch):
                c.wait(s, mc3["dx"])
                mm = s.activation(zsl[:, :], zsl[:, :], AF.Silu)
                mc3["sz"] = c.inc(mm, "sS")
            block.scalar(tail_s)

            def tail_v2(v, ch=ch):
                c.wait(v, mc3["sz"])
                v.tensor_mul(yall[:, :], yall[:, :], zsl[:, :])
                ln_stats(v, yall[:, :], 128, DI)
                v.tensor_mul(colw[:, 20:21], bnst[:, 48:49], bnst[:, 48:49])
                v.tensor_add(colw[:, 20:21], colw[:, 20:21], bnst[:, 49:50])
                v.tensor_scalar(colw[:, 20:21], colw[:, 20:21], EPS, None, ALU.add)
                mm = v.reciprocal(colw[:, 21:22], colw[:, 20:21])
                mc3["ys"] = c.inc(mm, "sV")
            block.vector(tail_v2)

            def tail_s2(s, ch=ch):
                c.wait(s, mc3["ys"])
                mm = s.activation(colw[:, 22:23], colw[:, 21:22], AF.Sqrt)
                mc3["rs"] = c.inc(mm, "sS")
            block.scalar(tail_s2)

            def tail_v3(v, ch=ch):
                c.wait(v, mc3["rs"])
                mm = v.tensor_scalar(xw[:, :], yall[:, :], colw[:, 22:23], None, ALU.mult)
                mc3["ygb"] = c.inc(mm, "sV")
            block.vector(tail_v3)

            if "d_y" in dump:
                def dyq(q, ch=ch):
                    c.wait(q, mc3["ygb"])
                    dy_marks[ch] = c.dinc(q.dma_start(out=d_y[ch * 128:(ch + 1) * 128, :],
                                                      in_=yall[:, :]), "dD")
                block.sync(dyq)

            for grp in range(4):
                def yt_t(te, ch=ch, grp=grp):
                    c.wait(te, mc3["ygb"])
                    c.pre_write(te, [3])
                    for u8 in range(8):
                        jj = grp * 8 + u8
                        mm = te.transpose(c.bank(3, u8 * 64, 64).bitcast(BF16),
                                          xw[:, jj * 128:(jj + 1) * 128], identB[:, :])
                    c.set_writer([3], c.inc(mm, "sT"))
                block.tensor(yt_t)

                def yt_v(v, ch=ch, grp=grp):
                    if grp == 0 and ch >= 1:
                        c.wait(v, yn_marks[ch - 1])
                    c.pre_read(v, [3])
                    mm = v.tensor_copy(ynstg[:, grp * 1024:(grp + 1) * 1024],
                                       c.bank(3, 0, 512).bitcast(BF16))
                    mk = c.inc(mm, "sV")
                    c.set_reader([3], mk)
                    mc3[("yn", grp)] = mk
                block.vector(yt_v)

            def yn_sync(q, ch=ch):
                for grp in range(4):
                    c.wait(q, mc3[("yn", grp)])
                mm = q.dma_start(out=ynorm_dram.transpose([1, 0, 2])[:, :, ch * 128:(ch + 1) * 128],
                                 in_=ynstg[:, :].rearrange("p (a e) -> p a e", a=XCT))
                yn_marks[ch] = c.dinc(mm, "dYN")
            block.sync(yn_sync)
        m_p3_done = yn_marks[NT - 1]

        if upto == "p3":
            _tail_stub(c, block, out, ident)
            stP3.close()
            stC.close()
            return nc

        stP3.close()
        stC.close()

        x2T = sbL("x2T", [128, KT * 1024], F32R)
        stP4 = ExitStack()
        attn_acc = sbL("attn_acc", [128, NT * DM], F32, stP4)
        ar = sbL("ar", [128, DM], F32, stP4)
        xmd = sbL("xmd", [128, DM], F32, stP4)
        x2b = sbL("x2b", [128, DM], F32, stP4)
        xst4 = sbL("xst4", [128, 2 * DM], F32, stP4)
        ynst = sbL("ynst", [128, 2 * 1024], BF16, stP4)
        wost = sbL("wost", [128, 2 * 512], F32, stP4)
        wob = sbL("wob", [128, 2 * 512], BF16, stP4)

        yn_rd = {}
        wo_rd = {}
        dr_marks = {}
        for dch in range(4):
            for k in range(XCT):
                ki = dch * XCT + k
                m4 = {}

                def p4_sync(q, dch=dch, k=k, ki=ki):
                    if ki >= 2:
                        c.wait(q, yn_rd[ki - 2])
                    if ki == 0:
                        c.wait(q, m_p3_done)
                    mm = q.dma_start(out=ynst[:, (ki % 2) * 1024:(ki % 2) * 1024 + 1024],
                                     in_=ynorm_dram[k, :, :])
                    m4["yn"] = c.dinc(mm, f"dYR{ki % 2}")
                    if ki >= 2:
                        c.wait(q, wo_rd[ki - 2])
                    mm = q.dma_start(out=wost[:, (ki % 2) * 512:(ki % 2) * 512 + 512],
                                     in_=wout_t[dch, k])
                    m4["wo"] = c.dinc(mm, f"dWO{ki % 2}")
                block.sync(p4_sync)

                def p4_g(v, k=k, ki=ki):
                    if ki == 0:
                        c.wait(v, m0["params"])
                    c.wait(v, m4["wo"])
                    if ki >= 2:
                        c.wait(v, wo_rd[ki - 2])
                    mm = v.tensor_scalar(wob[:, (ki % 2) * 512:(ki % 2) * 512 + 512],
                                         wost[:, (ki % 2) * 512:(ki % 2) * 512 + 512],
                                         normw_sb[:, k:k + 1], None, ALU.mult)
                    m4["wb"] = c.inc(mm, "sV")
                block.vector(p4_g)

                def p4_t(te, dch=dch, k=k, ki=ki):
                    c.wait(te, m4["yn"])
                    c.wait(te, m4["wb"])
                    for l in range(NT):
                        if k == 0:
                            c.pre_write(te, [l])
                        mm = te.matmul(c.bank(l, 0, 512),
                                       ynst[:, (ki % 2) * 1024 + l * 128:(ki % 2) * 1024 + (l + 1) * 128],
                                       wob[:, (ki % 2) * 512:(ki % 2) * 512 + 512],
                                       start=(k == 0), stop=(k == XCT - 1))
                    mk = c.inc(mm, "sT")
                    yn_rd[ki] = mk
                    wo_rd[ki] = mk
                    if k == XCT - 1:
                        c.set_writer(list(range(8)), mk)
                block.tensor(p4_t)

            def p4_dr(s, dch=dch):
                for l in range(NT):
                    c.pre_read(s, [l])
                    mm = s.activation(attn_acc[:, l * DM + dch * 512: l * DM + dch * 512 + 512],
                                      c.bank(l, 0, 512), AF.Identity)
                    mk = c.inc(mm, "sS")
                    c.set_reader([l], mk)
                    dr_marks[(dch, l)] = mk
            block.scalar(p4_dr)

        ep_marks = {}
        x2T_marks = {}
        xm_prev = [None]
        for l in range(NT):
            m5 = {}
            base = 4 * l

            def ep_sync(q, l=l):
                c.wait(q, dr_marks[(3, l)])
                mm = q.dma_start(out=xst4[:, (l % 2) * DM:(l % 2) * DM + DM],
                                 in_=xseg[3 + 128 * l: 3 + 128 * (l + 1), :])
                m5["x"] = c.dinc(mm, f"dX4{l % 2}")
                if l >= 2:
                    c.wait(q, ep_marks[l - 2])
            block.sync(ep_sync)

            def ep_v(v, l=l, base=base):
                c.wait(v, m5["x"])
                xin = xst4[:, (l % 2) * DM:(l % 2) * DM + DM]
                v.tensor_scalar(ar[:, :], xin, statc[:, base:base + 1],
                                statc[:, base + 1:base + 2], ALU.subtract, ALU.mult)
                v.tensor_add(ar[:, :], ar[:, :], attn_acc[:, l * DM:(l + 1) * DM])
                ln_stats(v, ar[:, :], 128, DM)
                v.tensor_scalar(colw[:, 24:25], bnst[:, 49:50], EPS, None, ALU.add)
                mm = v.reciprocal(colw[:, 25:26], colw[:, 24:25])
                m5["st2"] = c.inc(mm, "sV")
            block.vector(ep_v)

            def ep_s(s, l=l):
                c.wait(s, m5["st2"])
                mm = s.activation(colw[:, 26:27], colw[:, 25:26], AF.Sqrt)
                m5["rv2"] = c.inc(mm, "sS")
            block.scalar(ep_s)

            def ep_v2(v, l=l):
                c.wait(v, m5["rv2"])
                if l >= 1:
                    c.wait(v, xm_prev[0])
                xin = xst4[:, (l % 2) * DM:(l % 2) * DM + DM]
                v.tensor_scalar(ar[:, :], ar[:, :], bnst[:, 48:49], colw[:, 26:27],
                                ALU.subtract, ALU.mult)
                v.tensor_add(xmd[:, :], xin, ar[:, :])
                ln_stats(v, xmd[:, :], 128, DM)
                v.tensor_scalar(colw[:, 27:28], bnst[:, 49:50], EPS, None, ALU.add)
                mm = v.reciprocal(colw[:, 28:29], colw[:, 27:28])
                m5["st3"] = c.inc(mm, "sV")
            block.vector(ep_v2)

            def ep_sync2(q, l=l):
                c.wait(q, m5["st3"])
                m5["xm"] = c.dinc(q.dma_start(out=xmid_dram[128 * l:128 * (l + 1), :],
                                              in_=xmd[:, :]), "dXM")
                xm_prev[0] = m5["xm"]
            block.sync(ep_sync2)

            def ep_s2(s, l=l):
                c.wait(s, m5["st3"])
                mm = s.activation(colw[:, 29:30], colw[:, 28:29], AF.Sqrt)
                m5["rv3"] = c.inc(mm, "sS")
            block.scalar(ep_s2)

            def ep_v3(v, l=l):
                c.wait(v, m5["rv3"])
                mm = v.tensor_scalar(x2b[:, :], xmd[:, :], bnst[:, 48:49], colw[:, 29:30],
                                     ALU.subtract, ALU.mult)
                m5["x2"] = c.inc(mm, "sV")
            block.vector(ep_v3)

            def ep_t(te, l=l):
                c.wait(te, m5["x2"])
                for k in range(KT):
                    b = (l * KT + k) % 4
                    off = ((l * KT + k) // 4) % 4 * 128
                    c.pre_write(te, [b])
                    mm = te.transpose(c.bank(b, off, 128), x2b[:, k * 128:(k + 1) * 128],
                                      ident[:, :])
                    c.set_writer([b], c.inc(mm, "sT"))
                    m5[("tr", k)] = c.cur("sT")
            block.tensor(ep_t)

            def ep_v4(v, l=l):
                c.wait(v, m5[("tr", KT - 1)])
                last = None
                for k in range(KT):
                    b = (l * KT + k) % 4
                    off = ((l * KT + k) // 4) % 4 * 128
                    mm = v.tensor_copy(x2T[:, k * 1024 + l * 128: k * 1024 + (l + 1) * 128],
                                       c.bank(b, off, 128))
                    last = c.inc(mm, "sV")
                c.set_reader([0, 1, 2, 3], last)
                ep_marks[l] = last
                x2T_marks[l] = last
            block.vector(ep_v4)
        m_x2T_done = ep_marks[NT - 1]

        if upto == "p4":
            _tail_stub(c, block, out, ident)
            stP4.close()
            return nc

        stP4.close()
        y2acc = sbL("y2acc", [128, NT * DM], F32)
        stP5 = ExitStack()
        hq = sbL("hq", [128, 4 * 1024], BF16, stP5)
        ha = sbL("ha", [128, 4 * 1024], BF16, stP5)
        xcmp = sbL("xcmp", [128, 3 * 1024], BF16, stP5)
        gsil = sbL("gsil", [128, 1024], F32, stP5)
        kbuf = sbL("kbuf", [128, 1024], BF16, stP5)
        wgst = sbL("wgst", [128, 2 * 8 * 128], F32R, stP5)
        wust = sbL("wust", [128, 2 * 8 * 128], F32R, stP5)
        wdst = sbL("wdst", [128, 2048], F32, stP5)
        wdb = sbL("wdb", [128, 4 * 2048], BF16, stP5)


        gu_rd = {}
        gw_half = {}
        gu_half = {}
        wd_rd = {}
        hq_free = {}
        down_t = {}
        last_h = [None]
        for qd in range(16):
            its = [qd, qd + 16, qd + 32, qd + 48]
            mq = {}
            for si, it in enumerate(its):
                def f_sync(q, qd=qd, si=si, it=it):
                    idx = qd * 4 + si
                    if idx < 2:
                        c.wait(q, m_x2T_done)
                        c.wait(q, xm_prev[0])
                    for hf2 in range(2):
                        sx = hf2
                        if idx >= 1:
                            c.wait(q, gu_half.get((idx - 1, hf2)))
                        mm = q.dma_start(out=wgst[:, sx * 8 * 128:(sx + 1) * 8 * 128],
                                         in_=wg_p[it, hf2])
                        mq[("wg", si, hf2)] = c.dinc(mm, f"dWG{hf2}")
                        mm = q.dma_start(out=wust[:, sx * 8 * 128:(sx + 1) * 8 * 128],
                                         in_=wu_p[it, hf2])
                        mq[("wu", si, hf2)] = c.dinc(mm, f"dWU{hf2}")
                    if idx >= 1:
                        c.wait(q, wd_rd.get(idx - 1))
                    mm = q.dma_start(out=wdst[:, :],
                                     in_=wd_p[it * 128:(it + 1) * 128, :])
                    mq[("wd", si)] = c.dinc(mm, "dWD")
                block.sync(f_sync)

                def f_g(s, qd=qd, si=si):
                    idx = qd * 4 + si
                    if idx == 0:
                        c.wait(s, m_x2T_done)
                        c.wait(s, xm_prev[0])
                    c.wait(s, mq[("wd", si)])
                    if qd >= 1:
                        c.wait(s, down_t.get((qd - 1, NT - 1), None))
                    mm = s.activation(wdb[:, si * 2048:(si + 1) * 2048], wdst[:, :], AF.Identity)
                    wd_rd[idx] = c.inc(mm, "sS")
                    mq[("wdb", si)] = wd_rd[idx]
                block.scalar(f_g)

                def f_t(te, qd=qd, si=si):
                    idx = qd * 4 + si
                    if qd == 0 and si == 0:
                        c.wait(te, m_x2T_done)
                        c.wait(te, xm_prev[0])
                    if qd >= 1:
                        c.wait(te, hq_free.get(qd - 1))
                    c.pre_write(te, [4, 5])
                    for hf2 in range(2):
                        sx = hf2
                        c.wait(te, mq[("wg", si, hf2)])
                        for kk in range(8):
                            k = hf2 * 8 + kk
                            stat = wgst[:, sx * 8 * 128 + kk * 128: sx * 8 * 128 + (kk + 1) * 128]
                            for hx in range(2):
                                mm = te.matmul(c.bank(4 + hx, 0, 512), stat,
                                               x2T[:, k * 1024 + hx * 512: k * 1024 + (hx + 1) * 512],
                                               start=(k == 0), stop=(k == KT - 1))
                        gw_half[(idx, hf2)] = c.inc(mm, "sT")
                    mk = gw_half[(idx, 1)]
                    c.set_writer([4, 5], mk)
                block.tensor(f_t)

                def f_s(s, qd=qd, si=si):
                    if qd == 0 and si == 0:
                        c.wait(s, m_x2T_done)
                        c.wait(s, xm_prev[0])
                    if last_h[0] is not None:
                        c.wait(s, last_h[0])
                    c.pre_read(s, [4, 5])
                    mm = s.activation(gsil[:, :], c.ps[:, 4 * 512: 4 * 512 + 1024], AF.Silu)
                    mk = c.inc(mm, "sS")
                    c.set_reader([4, 5], mk)
                    mq[("gs", si)] = mk
                block.scalar(f_s)

                def f_t2(te, qd=qd, si=si):
                    idx = qd * 4 + si
                    c.wait(te, mq[("gs", si)])
                    c.pre_write(te, [6, 7])
                    for hf2 in range(2):
                        sx = hf2
                        c.wait(te, mq[("wu", si, hf2)])
                        for kk in range(8):
                            k = hf2 * 8 + kk
                            stat = wust[:, sx * 8 * 128 + kk * 128: sx * 8 * 128 + (kk + 1) * 128]
                            for hx in range(2):
                                mm = te.matmul(c.bank(6 + hx, 0, 512), stat,
                                               x2T[:, k * 1024 + hx * 512: k * 1024 + (hx + 1) * 512],
                                               start=(k == 0), stop=(k == KT - 1))
                        gu_half[(idx, hf2)] = c.inc(mm, "sT")
                    mk = gu_half[(idx, 1)]
                    c.set_writer([6, 7], mk)
                    gu_rd[idx] = mk
                block.tensor(f_t2)

                def f_v(v, qd=qd, si=si):
                    c.pre_read(v, [6, 7])
                    mm = v.tensor_tensor(hq[:, si * 1024:(si + 1) * 1024],
                                         c.ps[:, 6 * 512: 6 * 512 + 1024], gsil[:, :], ALU.mult)
                    mk = c.inc(mm, "sV")
                    c.set_reader([6, 7], mk)
                    mq[("h", si)] = mk
                    last_h[0] = mk
                block.vector(f_v)

            def sp_s(s, qd=qd):
                for si in range(4):
                    c.wait(s, mq[("h", si)])
                for si in range(4):
                    mm = s.activation(ha[:, si * 1024:(si + 1) * 1024],
                                      hq[:, si * 1024:(si + 1) * 1024], AF.Abs)
                mq["abs"] = c.inc(mm, "sS")
            block.scalar(sp_s)

            def sp_v(v, qd=qd):
                c.wait(v, mq["abs"])
                a = [ha[:, si * 1024:(si + 1) * 1024] for si in range(4)]
                x = [xcmp[:, i * 1024:(i + 1) * 1024] for i in range(3)]
                rk = sbtmp = None
                v.tensor_tensor(x[0], a[0], a[1], ALU.is_ge)
                v.tensor_tensor(x[1], a[0], a[2], ALU.is_ge)
                v.tensor_tensor(x[2], a[0], a[3], ALU.is_ge)
                v.tensor_add(a[0], x[0], x[1])
                v.tensor_add(a[0], a[0], x[2])
                v.tensor_scalar(x[0], x[0], -1.0, None, ALU.mult)
                v.tensor_scalar(x[1], x[1], -1.0, None, ALU.mult)
                v.tensor_scalar(x[2], x[2], -1.0, None, ALU.mult)
                v.tensor_tensor(kbuf[:, :], a[1], a[2], ALU.is_ge)
                v.scalar_tensor_tensor(x[1], kbuf[:, :], -1.0, x[1], ALU.mult, ALU.add)
                v.tensor_add(x[0], x[0], kbuf[:, :])
                v.tensor_tensor(kbuf[:, :], a[1], a[3], ALU.is_ge)
                v.tensor_add(x[0], x[0], kbuf[:, :])
                v.scalar_tensor_tensor(x[2], kbuf[:, :], -1.0, x[2], ALU.mult, ALU.add)
                v.tensor_tensor(kbuf[:, :], a[2], a[3], ALU.is_ge)
                v.tensor_add(x[1], x[1], kbuf[:, :])
                v.scalar_tensor_tensor(x[2], kbuf[:, :], -1.0, x[2], ALU.mult, ALU.add)
                ranks = [a[0], x[0], x[1], x[2]]
                thr = [1.5, 0.5, -0.5, -1.5]
                for si in range(4):
                    v.tensor_scalar(kbuf[:, :], ranks[si], thr[si], None, ALU.is_ge)
                    v.tensor_mul(hq[:, si * 1024:(si + 1) * 1024],
                                 hq[:, si * 1024:(si + 1) * 1024], kbuf[:, :])
                mq["sp"] = c.inc(v.tensor_copy(colw[:1, 30:31], ones[:1, 0:1]), "sV")
            block.vector(sp_v)

            for l in range(NT):
                def dn_t(te, qd=qd, l=l):
                    c.wait(te, mq["sp"])
                    for si in range(4):
                        c.wait(te, mq[("wdb", si)])
                    grp = [0, 1, 2, 3] if l % 2 == 0 else [4, 5, 6, 7]
                    if qd == 0 and l <= 1:
                        pass
                    c.pre_write(te, grp)
                    for si in range(4):
                        for dc in range(4):
                            mm = te.matmul(c.bank(grp[dc], 0, 512),
                                           hq[:, si * 1024 + l * 128: si * 1024 + (l + 1) * 128],
                                           wdb[:, si * 2048 + dc * 512: si * 2048 + (dc + 1) * 512],
                                           start=(si == 0), stop=(si == 3))
                    mk = c.inc(mm, "sT")
                    c.set_writer(grp, mk)
                    down_t[(qd, l)] = mk
                block.tensor(dn_t)

                def dn_v(v, qd=qd, l=l):
                    grp = [0, 1, 2, 3] if l % 2 == 0 else [4, 5, 6, 7]
                    c.pre_read(v, grp)
                    pslice = c.ps[:, grp[0] * 512: grp[0] * 512 + 2048]
                    dst = y2acc[:, l * DM:(l + 1) * DM]
                    if qd == 0:
                        mm = v.tensor_copy(dst, pslice)
                    else:
                        mm = v.tensor_add(dst, dst, pslice)
                    mk = c.inc(mm, "sV")
                    c.set_reader(grp, mk)
                    if l == NT - 1:
                        hq_free[qd] = mk
                block.vector(dn_v)
        m_ffn_done = c.cur("sV")

        out_marks = []
        for l in range(NT):
            m6 = {}

            def p6_sync(q, l=l):
                c.wait(q, m_ffn_done)
                if l >= 1:
                    c.wait(q, out_marks[l - 1])
                mm = q.dma_start(out=hq[:, :].bitcast(F32),
                                 in_=xmid_dram[128 * l:128 * (l + 1), :])
                m6["xm"] = c.dinc(mm, "dXR")
            block.sync(p6_sync)

            def p6_v(v, l=l):
                c.wait(v, m6["xm"])
                if l >= 2:
                    c.wait(v, out_marks[l - 2])
                ln_stats(v, y2acc[:, l * DM:(l + 1) * DM], 128, DM)
                v.tensor_scalar(colw[:, 24:25], bnst[:, 49:50], EPS, None, ALU.add)
                mm = v.reciprocal(colw[:, 25:26], colw[:, 24:25])
                m6["st"] = c.inc(mm, "sV")
            block.vector(p6_v)

            def p6_s(s, l=l):
                c.wait(s, m6["st"])
                mm = s.activation(colw[:, 26:27], colw[:, 25:26], AF.Sqrt)
                m6["rv"] = c.inc(mm, "sS")
            block.scalar(p6_s)

            def p6_v2(v, l=l):
                c.wait(v, m6["rv"])
                v.tensor_scalar(ha[:, :].bitcast(F32),
                                y2acc[:, l * DM:(l + 1) * DM], bnst[:, 48:49], colw[:, 26:27],
                                ALU.subtract, ALU.mult)
                mm = v.tensor_add(ha[:, :].bitcast(F32), ha[:, :].bitcast(F32),
                                  hq[:, :].bitcast(F32))
                m6["o"] = c.inc(mm, "sV")
            block.vector(p6_v2)

            def p6_sync2(q, l=l):
                c.wait(q, m6["o"])
                mm = q.dma_start(out=out[128 * l:128 * (l + 1), :],
                                 in_=ha[:, :].bitcast(F32))
                out_marks.append(c.dinc(mm, "dOUT"))
            block.sync(p6_sync2)

        def fin_sync(q):
            c.wait(q, out_marks[-1])
        block.sync(fin_sync)
        stP5.close()
        return nc


def _tail_stub(c, block, out, srcbuf):
    def syncf(q):
        c.wait(q, c.cur("sG"))
        c.wait(q, c.cur("sS"))
        c.wait(q, c.cur("sV"))
        c.wait(q, c.cur("sT"))
        c.wait(q, c.cur("sCC"))
        c.wait(q, c.cur("dD"))
        last = None
        for t in range(NT):
            last = c.dinc(q.dma_start(
                out=out[128 * t:128 * (t + 1), :].rearrange("p (a e) -> p a e", a=16),
                in_=srcbuf[:, 0:128].unsqueeze(1).broadcast_to([128, 16, 128])), "dOUT")
        c.wait(q, last)
    block.sync(syncf)


def host_layout(inputs, core):
    x = np.asarray(inputs["L0_attn_hidden_states"], np.float32).reshape(8192, 2048)
    r0 = core * T
    seg = np.zeros((3 + T, DM), np.float32)
    lo = max(0, r0 - 3)
    seg[3 - (r0 - lo):3] = x[lo:r0]
    seg[3:] = x[r0:r0 + T]
    win_t = np.zeros((DM, EIN + NH), np.float32)
    win_t[:, :EIN] = np.asarray(inputs["L0_attn_in_proj_w"], np.float32).T
    wtr = win_t.reshape(KT, 128, EIN + NH)
    e0s = [DI + 128 * j for j in range(XCT)] + [2 * DI, 2 * DI + 64, 2 * DI + 128]
    win_x = np.zeros((35, 2, 128, 1024), np.float32)
    for i, e0 in enumerate(e0s):
        blk = wtr[:, :, e0:e0 + 128]
        b2 = blk.reshape(2, 8, 128, 128).transpose(0, 2, 1, 3).reshape(2, 128, 1024)
        win_x[i] = b2
    wot = np.asarray(inputs["L0_attn_out_proj_w"], np.float32).T
    wout_t = np.ascontiguousarray(
        wot.reshape(32, 128, 4, 512).transpose(2, 0, 1, 3))
    perm = 4 * (np.arange(DFF) % 2048) + np.arange(DFF) // 2048

    def tile_gu(w):
        wp = np.asarray(w, np.float32)[:, perm]
        t = wp.reshape(2, 8, 128, 64, 128)
        return np.ascontiguousarray(t.transpose(3, 0, 2, 1, 4).reshape(64, 2, 128, 1024))
    wg_p = tile_gu(inputs["L0_ffn_w_gate"])
    wu_p = tile_gu(inputs["L0_ffn_w_up"])
    wd_p = np.ascontiguousarray(np.asarray(inputs["L0_ffn_w_down"], np.float32)[perm, :])
    cw = np.asarray(inputs["L0_attn_conv_w"], np.float32)
    cb = np.asarray(inputs["L0_attn_conv_b"], np.float32)
    convw = np.zeros((128, 34 * 4), np.float32)
    convb = np.zeros((128, 34), np.float32)
    for ct in range(32):
        convw[:, ct * 4:(ct + 1) * 4] = cw[ct * 128:(ct + 1) * 128]
        convb[:, ct] = cb[ct * 128:(ct + 1) * 128]
    convw[:64, 32 * 4:33 * 4] = cw[4096:4160]
    convb[:64, 32] = cb[4096:4160]
    convw[:64, 33 * 4:34 * 4] = cw[4160:4224]
    convb[:64, 33] = cb[4160:4224]
    rmsw_ = np.ascontiguousarray(np.asarray(inputs["L0_attn_rms_weight"], np.float32).reshape(KT, 128).T)
    normw_ = np.ascontiguousarray(np.asarray(inputs["L0_attn_norm_w"], np.float32).reshape(DI // 128, 128).T)
    return {
        "xseg": seg, "win_t": win_t, "win_x": win_x, "wout_t": wout_t, "wg_p": wg_p, "wu_p": wu_p, "wd_p": wd_p,
        "convw": convw, "convb": convb, "rmsw": rmsw_, "normw": normw_,
        "dtb": np.asarray(inputs["L0_attn_dt_bias"], np.float32).reshape(NH, 1),
        "alog": np.asarray(inputs["L0_attn_A_log"], np.float32).reshape(NH, 1),
        "dvec": np.asarray(inputs["L0_attn_D"], np.float32).reshape(NH, 1),
        "cmask": (np.arange(NCORE) < core).astype(np.float32).reshape(1, NCORE),
    }


def kernel(**inputs):
    nc = build_program("full")
    in_maps = [host_layout(inputs, core) for core in range(NCORE)]
    res = run_bass_kernel_spmd(nc, in_maps, core_ids=list(range(NCORE)))
    outp = np.concatenate([r["out"] for r in res.results], axis=0)
    return outp.reshape(1, 8192, 2048).astype(np.float32)
